# version 16
# baseline (speedup 1.0000x reference)
import contextlib
import numpy as np
import concourse.bass as bass
import concourse.mybir as mybir
from concourse.bass_utils import run_bass_kernel_spmd

F32 = mybir.dt.float32
AF = mybir.ActivationFunctionType
ALU = mybir.AluOpType

D = 1024
L = 4096
DIN = 2048
DTR = 64
T = 512
NCH = L // T
LNEPS = 1e-5

ENGS = ("sync", "tensor", "vector", "scalar", "gpsimd")
QSEM = {"sync": "qs", "gpsimd": "qg", "scalar": "qc", "vector": "qv"}
ESEM = {"tensor": "et", "vector": "ev", "scalar": "es", "gpsimd": "eg"}

N_SUBLAYERS = 4
ATTN_PHASES = "ABC"
B_LIMIT = 10 ** 9


class Plan:
    def __init__(self, nc):
        self.nc = nc
        self.ops = {e: [] for e in ENGS}
        self.cnt = {}
        self.sems = {}
        self.stack = contextlib.ExitStack()
        self.last_wait = {e: {} for e in ENGS}
        for s in set(QSEM.values()) | set(ESEM.values()):
            self.sems[s] = self.stack.enter_context(nc.semaphore(s))
            self.cnt[s] = 0

    def wait(self, eng, tok):
        if tok is None:
            return
        s, v = tok
        if v <= 0:
            return
        lw = self.last_wait[eng]
        if lw.get(s, -1) >= v:
            return
        lw[s] = v
        sem = self.sems[s]
        self.ops[eng].append(lambda e, sem=sem, v=v: e.wait_ge(sem, v))

    def now(self, eng_or_sem):
        s = ESEM.get(eng_or_sem) or QSEM.get(eng_or_sem) or eng_or_sem
        return (s, self.cnt[s])

    def stream_sem(self, name):
        if name not in self.sems:
            self.sems[name] = self.stack.enter_context(self.nc.semaphore(name))
            self.cnt[name] = 0
        return self.sems[name]

    def dma(self, q, out, in_, accum=None, waits=(), stream=None):
        qs = stream if stream is not None else QSEM[q]
        sem = self.stream_sem(qs)
        if q != "sync":
            es = ESEM.get(q)
            if es:
                self.wait(q, (es, self.cnt[es]))
        for t in waits:
            self.wait(q, t)
        self.cnt[qs] += 16
        v = self.cnt[qs]
        if accum is None:
            self.ops[q].append(lambda e, o=out, i=in_, sem=sem:
                               e.dma_start(out=o, in_=i).then_inc(sem, 16))
        else:
            self.ops[q].append(lambda e, o=out, i=in_, a=accum, sem=sem:
                               e.dma_start(out=o, in_=i, accum_op=a).then_inc(sem, 16))
        return (qs, v)

    def run(self, eng, fn, waits=(), inc=False):
        es = ESEM[eng]
        if eng != "tensor":
            self.wait(eng, (es, self.cnt[es]))
            inc = True
        for t in waits:
            self.wait(eng, t)
        if inc:
            self.cnt[es] += 1
            sem = self.sems[es]
            self.ops[eng].append(lambda e, fn=fn, sem=sem: fn(e).then_inc(sem, 1))
            return (es, self.cnt[es])
        self.ops[eng].append(lambda e, fn=fn: fn(e))
        return None

    def finalize(self, final_tokens):
        nc, plan = self.nc, self

        with nc.Block() as block:
            @block.sync
            def _(e):
                for fn in plan.ops["sync"]:
                    fn(e)
                for s, v in final_tokens:
                    e.wait_ge(plan.sems[s], v)

            @block.tensor
            def _(e):
                for fn in plan.ops["tensor"]:
                    fn(e)

            @block.vector
            def _(e):
                for fn in plan.ops["vector"]:
                    fn(e)

            @block.scalar
            def _(e):
                for fn in plan.ops["scalar"]:
                    fn(e)

            @block.gpsimd
            def _(e):
                for fn in plan.ops["gpsimd"]:
                    fn(e)


class Psum:
    def __init__(self, nc, plan):
        self.p = plan
        self.banks = [plan.stack.enter_context(nc.psum_tensor(f"pb{i}", [128, 512], F32))
                      for i in range(8)]
        self.free_tok = [None] * 8
        self.rr = 0

    def acquire(self, pool=None):
        while True:
            i = self.rr
            self.rr = (self.rr + 1) % 8
            if pool is None or i in pool:
                return i

    def group(self, idx, mms, extra_waits=()):
        p = self.p
        p.wait("tensor", self.free_tok[idx])
        for t in extra_waits:
            p.wait("tensor", t)
        tok, n = None, len(mms)
        for j, fn in enumerate(mms):
            last = j == n - 1
            tok = p.run("tensor", (lambda e, fn=fn, st=(j == 0), sp=last: fn(e, st, sp)),
                        inc=last)
        return tok

    def release(self, idx, tok):
        self.free_tok[idx] = tok


class Prog:
    def __init__(self, nsub=N_SUBLAYERS):
        nc = bass.Bass("TRN2", target_bir_lowering=False)
        self.nc = nc
        self.p = Plan(nc)
        self.ps = Psum(nc, self.p)
        self.nsub = nsub
        self.final_tokens = []
        self._build()

    def din(self, name, shape):
        return self.nc.dram_tensor(name, shape, F32, kind="ExternalInput").ap()

    def sbuf(self, name, cols, rows=128):
        return self.p.stack.enter_context(self.nc.sbuf_tensor(name, [rows, cols], F32))

    def emit_ln(self, tiles, t0, xtok, sq_area, extra_read_tok=None):
        p, ps = self.p, self.ps
        t_sqall = None
        for g in range(8):
            t_sqall = p.run("scalar",
                            lambda e, g=g: e.activation(sq_area(g), tiles[g], AF.Square),
                            waits=[xtok[g]] + ([extra_read_tok] if g == 0 else []),
                            inc=(g == 7))
        b0 = ps.acquire()
        pb0 = self.ps.banks[b0]
        t_sx = ps.group(b0, [
            (lambda e, st, sp, g=g: e.matmul(pb0[0:1, 0:T], self.ONES, tiles[g],
                                             start=st, stop=sp)) for g in range(8)],
            extra_waits=xtok)
        b1 = ps.acquire()
        pb1 = self.ps.banks[b1]
        t_sq = ps.group(b1, [
            (lambda e, st, sp, g=g: e.matmul(pb1[0:1, 0:T], self.ONES, sq_area(g),
                                             start=st, stop=sp)) for g in range(8)],
            extra_waits=[t_sqall])
        R = self.ROWQ
        mu, va, ex = (R[0:1, i * T:(i + 1) * T] for i in range(3))
        t1 = p.run("vector", lambda e: e.tensor_scalar_mul(mu, pb0[0:1, 0:T], 1.0 / D),
                   waits=[t_sx], inc=True)
        ps.release(b0, t1)
        t2 = p.run("vector", lambda e: e.tensor_scalar_mul(va, pb1[0:1, 0:T], 1.0 / D),
                   waits=[t_sq, self.prev_tb[0]], inc=True)
        ps.release(b1, t2)
        tq = p.run("scalar", lambda e: e.activation(ex, mu, AF.Square),
                   waits=[t1, self.prev_tb[1]], inc=True)
        p.run("vector", lambda e: e.tensor_sub(va, va, ex), waits=[tq])
        t3 = p.run("vector", lambda e: e.tensor_scalar_add(va, va, LNEPS), inc=True)
        t4 = p.run("scalar", lambda e: e.activation(ex, va, AF.Sqrt), waits=[t3], inc=True)
        t5 = p.run("vector", lambda e: e.reciprocal(va, ex), waits=[t4], inc=True)
        p.run("vector", lambda e: e.tensor_mul(ex, mu, va))
        t6 = p.run("vector", lambda e: e.tensor_scalar_mul(ex, ex, -1.0), inc=True)
        rstd, nmr = va, ex
        tb1 = p.dma("gpsimd", self.bounce[0:1, t0:t0 + T], rstd, waits=[t5],
                    stream="lnw")
        tb2 = p.dma("gpsimd", self.bounce[1:2, t0:t0 + T], nmr, waits=[t6],
                    stream="lnw")
        tb3 = p.dma("gpsimd", self.LNB[:, 0:T], self.brow(0, t0, T),
                    waits=[tb2, ("ev", p.cnt["ev"]),
                           ("eg", p.cnt["eg"])], stream="lnb")
        tb4 = p.dma("gpsimd", self.LNB[:, T:2 * T], self.brow(1, t0, T),
                    stream="lnb")
        tb3 = tb4
        self.prev_tb = (tb2, tb2)
        out_toks = []
        for g in range(8):
            p.run("vector", lambda e, g=g: e.tensor_mul(tiles[g], tiles[g], self.LNB[:, 0:T]),
                  waits=[tb4, t_sq, t_sqall])
            out_toks.append(p.run("vector", lambda e, g=g: e.tensor_add(
                tiles[g], tiles[g], self.LNB[:, T:2 * T]), waits=[tb4], inc=True))
        return out_toks

    def emit_mamba(self, src, dst, src_ready=None):
        p, ps, nc = self.p, self.ps, self.nc
        G1, GH, G3, G4, G5 = self.G1, self.GH, self.G3, self.G4, self.G5
        ZST = self.ZST
        SLT, XD, BCB, HC = self.SLT, self.XD, self.BCB, self.HC
        LNB = self.LNB
        HALO = self.HALO

        def g1(g):
            return G1[:, g * T:(g + 1) * T]

        def gh(g):
            return GH[:, g * (T + 3):(g + 1) * (T + 3)]

        def slab(S, g):
            return S[:, g * T:(g + 1) * T]

        carry_tok = [self.t_zero] * 256
        xn_free = [self.t_zero] * 8
        yacc_free = [self.t_zero] * 16
        g3_free = self.t_zero
        halo_restore = [self.t_zero] * 16
        wb_free = [None, None]
        slot = {}
        zslot_free = [None, None]
        zread_free = [None, None]
        scan_es_guard = None

        for c in range(NCH):
            t0 = c * T
            srdy = src_ready[c] if src_ready else []
            xtok = [p.dma("sync", g1(g), src[128 * g:128 * (g + 1), t0:t0 + T],
                          waits=[xn_free[g], scan_es_guard] + srdy, stream="xl")
                    for g in range(8)]
            xtok = [xtok[-1]] * 8
            for g in range(16):
                if c == 0:
                    halo_restore[g] = p.run("vector", lambda e, g=g: e.memset(
                        gh(g)[:, 0:3], 0.0), inc=True)
                else:
                    halo_restore[g] = p.run("vector", lambda e, g=g: e.tensor_copy(
                        gh(g)[:, 0:3], HALO[:, 3 * g:3 * g + 3]), inc=True)

            xn_tok = self.emit_ln([g1(g) for g in range(8)], t0, xtok,
                                  lambda g: slab(G3, g), extra_read_tok=g3_free)

            xin_evac = [None] * 16
            for m in range(32):
                reg = m % 2
                wtok, waps = self.load_wblocks(reg, self.w_in_t, list(range(8)),
                                               128 * m, 128, wb_free)
                b = ps.acquire()
                pb = ps.banks[b]
                mms = [(lambda e, st, sp, k=k, ap=waps[k], pb=pb: e.matmul(
                    pb[:, 0:T], ap, g1(k), start=st, stop=sp)) for k in range(8)]
                tok = ps.group(b, mms, extra_waits=[wtok] + xn_tok)
                wb_free[reg] = tok
                if m < 16:
                    g = m
                    te = p.run("scalar", lambda e, g=g, pb=pb: e.activation(
                        gh(g)[:, 3:3 + T], pb[:, 0:T], AF.Identity, bias=self.BINCOL(g)),
                        waits=[tok, halo_restore[g], yacc_free[g]], inc=True)
                    xin_evac[g] = te
                else:
                    g = m - 16
                    zs = ZST[:, (g % 2) * T:(g % 2 + 1) * T]
                    te = p.run("scalar", lambda e, zs=zs, pb=pb, m=m: e.activation(
                        zs, pb[:, 0:T], AF.Silu, bias=self.BINCOL(m)),
                        waits=[tok, zslot_free[g % 2],
                               slot.get(("z", 2 + g % 2))], inc=True)
                    zslot_free[g % 2] = p.dma(
                        "gpsimd", self.zsilu[128 * g:128 * (g + 1), t0:t0 + T], zs,
                        waits=[te], stream=f"zw{g % 2}")
                ps.release(b, te)
            last_xz = p.now("tensor")
            for g in range(8):
                xn_free[g] = last_xz

            conv_s_tok = [None] * 16
            for g in range(16):
                p.run("vector", lambda e, g=g: e.tensor_scalar(
                    slab(G3, g), gh(g)[:, 0:T], self.TAP(g, 0), None, ALU.mult),
                    waits=[xin_evac[g]])
                for j in (1, 2, 3):
                    p.run("vector", lambda e, g=g, j=j: e.scalar_tensor_tensor(
                        slab(G3, g), gh(g)[:, j:j + T], self.TAP(g, j), slab(G3, g),
                        ALU.mult, ALU.add))
                tcv = p.run("vector", lambda e, g=g: e.tensor_copy(
                    HALO[:, 3 * g:3 * g + 3], gh(g)[:, T:T + 3]), inc=True)
                conv_s_tok[g] = p.run("scalar", lambda e, g=g: e.activation(
                    slab(G3, g), slab(G3, g), AF.Silu, bias=self.CBCOL(g)),
                    waits=[tcv], inc=True)
            t_halo_saved = p.now("vector")

            b = ps.acquire()
            pb = ps.banks[b]
            mmx = [(lambda e, st, sp, k=k, pb=pb: e.matmul(
                pb[0:96, 0:T], self.WXT[:, 96 * k:96 * (k + 1)], slab(G3, k),
                start=st, stop=sp)) for k in range(16)]
            tok = ps.group(b, mmx, extra_waits=conv_s_tok + [self.wxt_tok])
            t_xd = p.run("scalar", lambda e, pb=pb: e.activation(
                XD[0:96, 0:T], pb[0:96, 0:T], AF.Identity, bias=self.BXCOL),
                waits=[tok], inc=True)
            ps.release(b, t_xd)
            tbB = p.dma("gpsimd", self.bounce[2:18, t0:t0 + T], XD[64:80, 0:T],
                        waits=[t_xd], stream="bcw")
            tbC = p.dma("gpsimd", self.bounce[18:34, t0:t0 + T], XD[80:96, 0:T],
                        stream="bcw")
            tbB = tbC

            dt_tok = [None] * 16
            for m in range(16):
                b = ps.acquire()
                pb = ps.banks[b]
                tok = ps.group(b, [(lambda e, st, sp, m=m, pb=pb: e.matmul(
                    pb[:, 0:T], self.WDT[0:64, 128 * m:128 * (m + 1)], XD[0:64, 0:T],
                    start=st, stop=sp))], extra_waits=[t_xd, self.wdt_tok])
                p.run("scalar", lambda e, m=m, pb=pb: e.activation(
                    slab(G4, m), pb[:, 0:T], AF.Exp, bias=self.BDTCOL(m)),
                    waits=[tok])
                dt_tok[m] = p.run("scalar", lambda e, m=m: e.activation(
                    slab(G4, m), slab(G4, m), AF.Ln, bias=1.0), inc=True)
                ps.release(b, dt_tok[m])

            dtx_tok = [None] * 16
            for g in range(16):
                dtx_tok[g] = p.run("vector", lambda e, g=g: e.tensor_mul(
                    slab(G5, g), slab(G4, g), slab(G3, g)),
                    waits=[dt_tok[g], conv_s_tok[g]], inc=True)

            zstg = [("zw0", p.cnt.get("zw0", 0)), ("zw1", p.cnt.get("zw1", 0)),
                    ("zr0", p.cnt.get("zr0", 0)), ("zr1", p.cnt.get("zr1", 0))]
            yadd_tok = [[None] * 16 for _ in range(16)]
            pend_yadd = []
            u_last2 = [p.now("vector"), p.now("vector")]
            z_last2 = [p.now("vector"), p.now("vector")]
            for s in range(16):
                sp2 = s % 2
                bbuf = BCB[:, 0:T] if sp2 == 0 else LNB[:, 0:T]
                cbuf = BCB[:, T:2 * T] if sp2 == 0 else LNB[:, T:2 * T]
                tb_B = p.dma("gpsimd", bbuf,
                             self.brow(2 + s, t0, T), waits=[tbB, u_last2[sp2]],
                             stream="bcbB%d" % sp2)
                tb_C = p.dma("gpsimd", cbuf,
                             self.brow(18 + s, t0, T),
                             waits=[tbC, z_last2[sp2]], stream="bcbC%d" % sp2)
                for gb in range(8):
                    gl = [2 * gb + j for j in range(2)]
                    igl = [s * 16 + g for g in gl]
                    ipl = [i % 4 for i in igl]
                    dAl = [G1[:, ip * T:(ip + 1) * T] for ip in ipl]
                    ul = [G1[:, (4 + ip) * T:(5 + ip) * T] for ip in ipl]
                    hl = [SLT[:, (2 + ip) * T:(3 + ip) * T] for ip in ipl]
                    zl = [(SLT[:, (6 + ip) * T:(7 + ip) * T] if ip < 2
                           else ZST[:, (ip - 2) * T:(ip - 1) * T]) for ip in ipl]
                    t_dal, t_ul, t_hl = [], [], []
                    for j in range(len(gl)):
                        g, i_glob, ip = gl[j], igl[j], ipl[j]
                        t_dal.append(p.run("scalar",
                                     lambda e, g=g, s=s, dA=dAl[j]: e.activation(
                                         dA, slab(G4, g), AF.Exp,
                                         scale=self.ACOL(g, s)),
                                     waits=[self.t_acol, dt_tok[g],
                                            slot.get(("dA", ip)),
                                            last_xz if i_glob < 4 else None],
                                     inc=True))
                    for j in range(len(gl)):
                        g, i_glob, ip = gl[j], igl[j], ipl[j]
                        t_ul.append(p.run("vector",
                                    lambda e, g=g, u=ul[j], bbuf=bbuf: e.tensor_mul(
                                        u, slab(G5, g), bbuf),
                                    waits=[dtx_tok[g], tb_B, slot.get(("u", ip)),
                                           last_xz if i_glob < 4 else None],
                                    inc=True))
                        u_last2[sp2] = t_ul[-1]
                    for (pg, psn, pz1, ptz) in pend_yadd:
                        if psn == 0:
                            tk = p.dma("gpsimd", gh(pg)[:, 3:3 + T], pz1,
                                       waits=[ptz, yacc_free[pg]],
                                       stream=f"ya{pg}")
                        else:
                            tk = p.dma("gpsimd", gh(pg)[:, 3:3 + T], pz1,
                                       accum=ALU.add,
                                       waits=[ptz, yadd_tok[pg][psn - 1]],
                                       stream=f"ya{pg}")
                        yadd_tok[pg][psn] = tk
                        slot[("z", (16 * psn + pg) % 4)] = tk
                    pend_yadd = []
                    for j in range(len(gl)):
                        g, i_glob, ip = gl[j], igl[j], ipl[j]
                        idx = g * 16 + s
                        t_hl.append(p.run("vector",
                                    lambda e, dA=dAl[j], u=ul[j], h=hl[j],
                                    idx=idx: e.tensor_tensor_scan(
                                        h, dA, u, HC[:, idx:idx + 1],
                                        ALU.mult, ALU.add),
                                    waits=[t_dal[j], t_ul[j], carry_tok[idx],
                                           slot.get(("h", ip)),
                                           slot.get(("u", ip % 2))
                                           if i_glob < 4 else None], inc=True))
                        slot[("dA", ip)] = t_hl[j]
                        slot[("u", ip)] = t_hl[j]
                    for j in range(len(gl)):
                        g, i_glob, ip = gl[j], igl[j], ipl[j]
                        idx = g * 16 + s
                        t_cy = p.run("scalar", lambda e, h=hl[j], idx=idx: e.copy(
                            HC[:, idx:idx + 1], h[:, T - 1:T]),
                            waits=[t_hl[j]], inc=True)
                        carry_tok[idx] = t_cy
                        slot[("h", ip)] = t_cy
                    for j in range(len(gl)):
                        g, i_glob, ip = gl[j], igl[j], ipl[j]
                        t_z = p.run("vector", lambda e, h=hl[j], z1=zl[j],
                                    cbuf=cbuf: e.tensor_mul(z1, h, cbuf),
                                    waits=[tb_C, slot.get(("z", ip)),
                                           zstg[0] if ip >= 2 else None,
                                           zstg[1] if ip >= 2 else None,
                                           zstg[2] if ip >= 2 else None,
                                           zstg[3] if ip >= 2 else None],
                                    inc=True)
                        pend_yadd.append((g, s, zl[j], t_z))
                        z_last2[sp2] = t_z
            for (pg, psn, pz1, ptz) in pend_yadd:
                tk = p.dma("gpsimd", gh(pg)[:, 3:3 + T], pz1, accum=ALU.add,
                           waits=[ptz, yadd_tok[pg][psn - 1]], stream=f"ya{pg}")
                yadd_tok[pg][psn] = tk
                slot[("z", (16 * psn + pg) % 4)] = tk
            pend_yadd = []
            scan_done_ev = p.now("vector")
            scan_es_guard = p.now("scalar")
            for g in range(8):
                xn_free[g] = scan_done_ev
            yfin_tok = []
            for g in range(16):
                zs = ZST[:, (g % 2) * T:(g % 2 + 1) * T]
                tzl = p.dma("sync", zs, self.zsilu[128 * g:128 * (g + 1), t0:t0 + T],
                            waits=[zslot_free[g % 2], zread_free[g % 2],
                                   slot.get(("z", 2 + g % 2))],
                            stream=f"zr{g % 2}")
                p.run("vector", lambda e, g=g: e.scalar_tensor_tensor(
                    gh(g)[:, 3:3 + T], slab(G3, g), self.DCOL(g), gh(g)[:, 3:3 + T],
                    ALU.mult, ALU.add), waits=[yadd_tok[g][15]])
                tm = p.run("vector", lambda e, g=g, zs=zs: e.tensor_mul(
                    gh(g)[:, 3:3 + T], gh(g)[:, 3:3 + T], zs), waits=[tzl], inc=True)
                yfin_tok.append(tm)
                zread_free[g % 2] = tm
            self.t_g3w = p.now("vector")
            g3_free = self.t_g3w

            all_yadds = None
            yfin_now = p.now("vector")
            for m in range(8):
                b = ps.acquire()
                pb = ps.banks[b]
                p.wait("tensor", ps.free_tok[b])
                for t in yfin_tok:
                    p.wait("tensor", t)
                tok = None
                for half in range(2):
                    reg = (2 * m + half) % 2
                    wtok, waps = self.load_wblocks(reg, self.w_out_t,
                                                   list(range(8 * half, 8 * half + 8)),
                                                   128 * m, 128, wb_free)
                    p.wait("tensor", wtok)
                    for j in range(8):
                        k = 8 * half + j
                        tok = p.run("tensor", lambda e, k=k, ap=waps[j],
                                    st=(k == 0), sp=(half == 1 and j == 7),
                                    pb=pb: e.matmul(
                            pb[:, 0:T], ap, gh(k)[:, 3:3 + T], start=st, stop=sp),
                            inc=(j == 7))
                    wb_free[reg] = tok
                ta = SLT[:, (0 + m % 2) * T:(1 + m % 2) * T]
                xr = SLT[:, (2 + m % 2) * T:(3 + m % 2) * T]
                txr = p.dma("sync", xr, src[128 * m:128 * (m + 1), t0:t0 + T],
                            waits=[yfin_now, slot.get(("u", m % 2)),
                                   slot.get(("h", m % 2))] + srdy,
                            stream=f"xr{m % 2}")
                te = p.run("scalar", lambda e, pb=pb, m=m, ta=ta: e.activation(
                    ta, pb[:, 0:T], AF.Identity, bias=self.BOUTCOL(m)),
                    waits=[tok, slot.get(("z", m % 2)), slot.get(("dA", m % 2))],
                    inc=True)
                ps.release(b, te)
                tv = p.run("vector", lambda e, ta=ta, xr=xr: e.tensor_add(ta, ta, xr),
                           waits=[te, txr], inc=True)
                tst = p.dma("gpsimd", dst[128 * m:128 * (m + 1), t0:t0 + T], ta,
                            waits=[tv], stream=f"st{m % 2}")
                slot[("dA", m % 2)] = tst
                slot[("u", m % 2)] = tv
                if c == NCH - 1:
                    self.final_tokens.append(tst)
            self.store_toks.append([("st0", p.cnt.get("st0", 0)),
                                    ("st1", p.cnt.get("st1", 0))])
            t_lastw = p.now("tensor")
            for g in range(16):
                yacc_free[g] = t_lastw

    def load_wblocks(self, reg, wsrc, klist, m0, mwidth, wb_free):
        p = self.p
        aps, tok = [], None
        for j, k in enumerate(klist):
            dst = self.WB[:, reg * 1024 + j * mwidth: reg * 1024 + (j + 1) * mwidth]
            tok = p.dma("sync", dst, wsrc[128 * k:128 * (k + 1), m0:m0 + mwidth],
                        waits=[wb_free[reg]] if j == 0 else (),
                        stream=f"wb{reg}")
            aps.append(dst)
        return tok, aps


    def emit_ffn(self, src, dst, is_last, tag, src_ready=None):
        p, ps, nc = self.p, self.ps, self.nc
        G1, GH, G3, G4, G5 = self.G1, self.GH, self.G3, self.G4, self.G5
        SLT, XD, BCB, LNB, WB = self.SLT, self.XD, self.BCB, self.LNB, self.WB
        AX = mybir.AxisListType

        def g1(g):
            return G1[:, g * T:(g + 1) * T]

        def gh(g):
            return GH[:, g * (T + 3) + 3:g * (T + 3) + 3 + T]

        def slab(S, g):
            return S[:, g * T:(g + 1) * T]

        def dacc(mo):
            return G4[:, (4 + mo) * T:(5 + mo) * T]

        def comb(m):
            return G3[:, (8 + m) * T:(9 + m) * T]

        bnd = [p.now(k) for k in ("et", "ev", "es", "eg")]
        bcw_b = ("bcw", p.cnt.get("bcw", 0))
        xn_free = [bnd[0]] * 8
        g3_free = bnd[1]
        wslot = [None] * 4
        lnb_free = [bnd[1], bcw_b]
        rslot = {("ta", 0): ("st0", p.cnt.get("st0", 0)),
                 ("ta", 1): ("st1", p.cnt.get("st1", 0)),
                 ("xr", 0): ("xr0", p.cnt.get("xr0", 0)),
                 ("xr", 1): ("xr1", p.cnt.get("xr1", 0)),
                 ("rs", 0): bnd[1], ("rs", 1): bnd[1]}
        routed_rd = [bnd[0]] * 8
        comb_rd = bnd[0]
        lat_rd = [bnd[3]] * 4
        g5_rd = bnd[3]
        dacc_rd = [bnd[1]] * 8
        xd_rd = [bnd[0], bcw_b]
        wexp_rd = [bnd[3], bnd[1]]

        def wload(si, src_ap, waits=()):
            dstp = WB[:, si * 512:(si + 1) * 512]
            tok = p.dma("sync", dstp, src_ap,
                        waits=[wslot[si]] + list(waits), stream="ws%d%s" % (si, tag))
            return tok, dstp

        for c in range(NCH):
            t0 = c * T
            srdy = src_ready[c] if src_ready else []
            xtok = [p.dma("sync", g1(g), src[128 * g:128 * (g + 1), t0:t0 + T],
                          waits=[xn_free[g]] + srdy, stream="xl") for g in range(8)]
            xtok = [xtok[-1]] * 8
            xn_tok = self.emit_ln([g1(g) for g in range(8)], t0, xtok,
                                  lambda g: slab(G3, g), extra_read_tok=g3_free)

            b0 = ps.acquire()
            pb0 = ps.banks[b0]
            t_mm = ps.group(b0, [
                (lambda e, st, sp, k=k, pb0=pb0: e.matmul(
                    pb0[0:8, 0:T], self.GATE[:, 8 * k:8 * (k + 1)], g1(k),
                    start=st, stop=sp)) for k in range(8)],
                extra_waits=xn_tok + [self.ini_tok])
            t_sc = p.run("scalar", lambda e, pb0=pb0: e.activation(
                XD[0:8, 0:T], pb0[0:8, 0:T], AF.Sigmoid),
                waits=[t_mm] + xd_rd, inc=True)
            ps.release(b0, t_sc)

            bT = ps.acquire()
            pbT = ps.banks[bT]
            bW = ps.acquire()
            pbW = ps.banks[bW]
            p.wait("tensor", ps.free_tok[bT])
            p.wait("tensor", ps.free_tok[bW])
            w_toks = []
            prev_tv = None
            for j in range(4):
                tj = p.run("tensor", lambda e, j=j, pbT=pbT: e.transpose(
                    pbT[:, 8 * j:8 * j + 8], XD[0:8, 128 * j:128 * (j + 1)],
                    self.IDT[0:8, 0:8]), waits=[t_sc, prev_tv], inc=True)
                base = (j % 2) * T
                sct = SLT[:, base:base + 8]
                sfc = SLT[:, base + 8:base + 16]
                msk = SLT[:, base + 16:base + 24]
                wk = SLT[:, base + 24:base + 32]
                c1 = SLT[:, base + 32:base + 33]
                c2 = SLT[:, base + 33:base + 34]
                c3 = SLT[:, base + 34:base + 35]
                tv = p.run("vector", lambda e, sct=sct, pbT=pbT, j=j: e.tensor_copy(
                    sct, pbT[:, 8 * j:8 * j + 8]),
                    waits=[tj, rslot.get(("rt", j % 2)), rslot.get(("ta", j % 2)),
                           rslot.get(("xr", j % 2))])
                prev_tv = tv
                p.run("vector", lambda e, sfc=sfc, sct=sct: e.tensor_add(
                    sfc, sct, self.CORR), waits=[self.ini_tok])
                for gi in (0, 1):
                    gsl = sfc[:, 4 * gi:4 * gi + 4]
                    mg = msk[:, 4 * gi:4 * gi + 4]
                    gc = c1 if gi == 0 else c2
                    p.run("vector", lambda e, gsl=gsl, c3=c3: e.reduce_max(
                        c3, gsl, axis=AX.X))
                    p.run("vector", lambda e, gsl=gsl, mg=mg, c3=c3: e.tensor_scalar(
                        mg, gsl, c3, None, ALU.is_equal))
                    p.run("vector", lambda e, gsl=gsl, mg=mg: e.scalar_tensor_tensor(
                        mg, mg, -1e30, gsl, ALU.mult, ALU.add))
                    p.run("vector", lambda e, mg=mg, gc=gc: e.reduce_max(
                        gc, mg, axis=AX.X))
                    p.run("vector", lambda e, gc=gc, c3=c3: e.tensor_add(gc, gc, c3))
                p.run("vector", lambda e, c1=c1, c2=c2, c3=c3: e.tensor_tensor(
                    c3, c1, c2, ALU.is_ge))
                p.run("vector", lambda e, msk=msk, c3=c3: e.tensor_scalar(
                    msk[:, 0:4], self.ONES8[:, 0:4], c3, None, ALU.mult))
                p.run("vector", lambda e, msk=msk, c3=c3: e.tensor_scalar(
                    msk[:, 4:8], self.ONES8[:, 4:8], c3, None, ALU.subtract))
                p.run("vector", lambda e, sfc=sfc, msk=msk: e.tensor_mul(
                    sfc, sfc, msk))
                p.run("vector", lambda e, sfc=sfc, c1=c1: e.reduce_max(
                    c1, sfc, axis=AX.X))
                p.run("vector", lambda e, sfc=sfc, msk=msk, c1=c1: e.tensor_scalar(
                    msk, sfc, c1, None, ALU.is_equal))
                p.run("vector", lambda e, sfc=sfc, msk=msk: e.scalar_tensor_tensor(
                    msk, msk, -1e30, sfc, ALU.mult, ALU.add))
                p.run("vector", lambda e, msk=msk, c2=c2: e.reduce_max(
                    c2, msk, axis=AX.X))
                p.run("vector", lambda e, sfc=sfc, msk=msk, c2=c2: e.tensor_scalar(
                    msk, sfc, c2, None, ALU.is_ge))
                p.run("vector", lambda e, wk=wk, msk=msk, sct=sct: e.tensor_mul(
                    wk, msk, sct))
                p.run("vector", lambda e, wk=wk, c1=c1: e.reduce_sum(
                    c1, wk, axis=AX.X))
                p.run("vector", lambda e, c1=c1: e.tensor_scalar_add(c1, c1, 1e-20))
                p.run("vector", lambda e, c1=c1, c2=c2: e.reciprocal(c2, c1))
                tw = p.run("vector", lambda e, wk=wk, c2=c2: e.tensor_scalar(
                    wk, wk, c2, None, ALU.mult))
                tj2 = p.run("tensor", lambda e, j=j, pbW=pbW, wk=wk: e.transpose(
                    pbW[0:8, 128 * j:128 * (j + 1)], wk, self.IDT[:, :]),
                    waits=[tw], inc=True)
                w_toks.append(tj2)
                rslot[("rt", j % 2)] = tj2
            t_wev = p.run("scalar", lambda e, pbW=pbW: e.activation(
                BCB[0:8, 0:T], pbW[0:8, 0:T], AF.Copy),
                waits=w_toks + wexp_rd, inc=True)
            ps.release(bT, t_wev)
            ps.release(bW, t_wev)
            t_wb = p.dma("gpsimd", self.bounce[34:42, t0:t0 + T], BCB[0:8, 0:T],
                         waits=[t_wev], stream="wxw")
            wexp_rd = [t_wb]

            bl = [ps.acquire(pool=(0, 1, 2, 3)) for _ in range(4)]
            gtok = None
            for k in range(8):
                wtok, wap = wload(k % 2, self.fc1_t[128 * k:128 * (k + 1), 0:512])
                for m in range(4):
                    if k == 0:
                        p.wait("tensor", ps.free_tok[bl[m]])
                    gtok = p.run("tensor",
                                 lambda e, m=m, k=k, wap=wap, pbx=ps.banks[bl[m]]:
                                 e.matmul(pbx[:, 0:T],
                                          wap[:, 128 * m:128 * (m + 1)], g1(k),
                                          start=(k == 0), stop=(k == 7)),
                                 waits=[wtok] + (xn_tok if k == 0 else []),
                                 inc=(m == 3))
                wslot[k % 2] = ("et", p.cnt["et"])
            lat_tok = [None] * 4
            for m in range(4):
                lat_tok[m] = p.run("scalar", lambda e, m=m, pbx=ps.banks[bl[m]]:
                                   e.activation(slab(G4, m), pbx[:, 0:T], AF.Copy),
                                   waits=[gtok, lat_rd[m]], inc=True)
                ps.release(bl[m], lat_tok[m])

            rb = [0, 1, 2, 3]
            cb = [4, 5, 6, 7]
            for bb in cb:
                p.wait("tensor", ps.free_tok[bb])
            ctok = None
            rinfo = None

            def emit_rphase(ei):
                epar = ei % 2
                base = 4 * epar
                twb = p.dma("gpsimd", LNB[:, (ei % 2) * T:(ei % 2 + 1) * T],
                            self.brow(34 + ei, t0, T),
                            waits=[t_wb, lnb_free[ei % 2]], stream="web%d" % (ei % 2))
                rtok = None
                for k in range(4):
                    wtok, wap = wload(k % 2, self.w1f[512 * ei + 128 * k:
                                                      512 * ei + 128 * (k + 1), 0:512])
                    for m in range(4):
                        if k == 0:
                            p.wait("tensor", ps.free_tok[rb[m]])
                        rtok = p.run("tensor",
                                     lambda e, m=m, k=k, wap=wap,
                                     pbx=ps.banks[rb[m]]: e.matmul(
                                         pbx[:, 0:T],
                                         wap[:, 128 * m:128 * (m + 1)],
                                         slab(G4, k), start=(k == 0), stop=(k == 3)),
                                     waits=[wtok, lat_tok[k]],
                                     inc=(m == 3))
                    wslot[k % 2] = ("et", p.cnt["et"])
                g5w = [None] * 4
                for m in range(4):
                    rs = SLT[:, (4 + m % 2) * T:(5 + m % 2) * T]
                    t1 = p.run("scalar", lambda e, rs=rs, pbx=ps.banks[rb[m]]:
                               e.activation(rs, pbx[:, 0:T], AF.Relu),
                               waits=[rtok, rslot.get(("rs", m % 2))], inc=True)
                    ps.release(rb[m], t1)
                    p.run("scalar", lambda e, rs=rs: e.activation(
                        rs, rs, AF.Square))
                    g5w[m] = p.run("vector", lambda e, rs=rs, m=m, ei=ei,
                                   base=base: e.tensor_mul(
                                       slab(G5, base + m), rs,
                                       LNB[:, (ei % 2) * T:(ei % 2 + 1) * T]),
                                   waits=[("es", p.cnt["es"]), twb,
                                          g5_rd if ei < 2 else None,
                                          slot_g5.get(epar)], inc=True)
                    rslot[("rs", m % 2)] = g5w[m]
                lnb_free[ei % 2] = ("ev", p.cnt["ev"])
                return (ei, g5w, base)

            def emit_combphase(info):
                nonlocal ctok
                ei, g5w, base = info
                for k in range(4):
                    wtok, wap = wload(2 + k % 2,
                                      self.w2f[512 * ei + 128 * k:
                                               512 * ei + 128 * (k + 1), 0:512])
                    for m in range(4):
                        ctok = p.run("tensor",
                                     lambda e, m=m, k=k, wap=wap, base=base,
                                     st_=(ei == 0 and k == 0),
                                     sp_=(ei == 7 and k == 3),
                                     pbx=ps.banks[cb[m]]: e.matmul(
                                         pbx[:, 0:T],
                                         wap[:, 128 * m:128 * (m + 1)],
                                         slab(G5, base + k),
                                         start=st_, stop=sp_),
                                     waits=[wtok, g5w[k]],
                                     inc=(m == 3))
                    wslot[2 + k % 2] = ("et", p.cnt["et"])
                slot_g5[ei % 2] = ("et", p.cnt["et"])

            slot_g5 = {}
            for ei in range(8):
                nxt = emit_rphase(ei)
                if rinfo is not None:
                    emit_combphase(rinfo)
                rinfo = nxt
            emit_combphase(rinfo)
            comb_tok = [None] * 4
            for m in range(4):
                comb_tok[m] = p.run("scalar", lambda e, m=m, pbx=ps.banks[cb[m]]:
                                    e.activation(comb(m), pbx[:, 0:T], AF.Copy),
                                    waits=[ctok, comb_rd], inc=True)
                ps.release(cb[m], comb_tok[m])

            for mog in range(2):
                bf = [ps.acquire(pool=(0, 1, 2, 3)) for _ in range(4)]
                ftok = None
                for k in range(4):
                    wtok, wap = wload(k % 2, self.fc2_t[128 * k:128 * (k + 1),
                                                        512 * mog:512 * (mog + 1)])
                    for m in range(4):
                        if k == 0:
                            p.wait("tensor", ps.free_tok[bf[m]])
                        ftok = p.run("tensor",
                                     lambda e, m=m, k=k, wap=wap,
                                     pbx=ps.banks[bf[m]]: e.matmul(
                                         pbx[:, 0:T],
                                         wap[:, 128 * m:128 * (m + 1)],
                                         comb(k), start=(k == 0), stop=(k == 3)),
                                     waits=[wtok, comb_tok[k]],
                                     inc=(m == 3))
                    wslot[k % 2] = ("et", p.cnt["et"])
                for m in range(4):
                    mo = 4 * mog + m
                    tr = p.run("scalar", lambda e, mo=mo, pbx=ps.banks[bf[m]]:
                               e.activation(gh(mo), pbx[:, 0:T], AF.Copy),
                               waits=[ftok, routed_rd[mo]], inc=True)
                    ps.release(bf[m], tr)
                    routed_rd[mo] = tr
            comb_rd = ("et", p.cnt["et"])

            store_toks = []
            up_tok = [None] * 16
            for half in range(2):
                for mug in range(4):
                    mub = 4 * half + mug
                    bu = [ps.acquire(pool=(0, 1, 2, 3)) for _ in range(4)]
                    utok = None
                    for k in range(8):
                        wtok, wap = wload(k % 2,
                                          self.up_t[128 * k:128 * (k + 1),
                                                    512 * mub:512 * (mub + 1)])
                        for m in range(4):
                            if k == 0:
                                p.wait("tensor", ps.free_tok[bu[m]])
                            utok = p.run("tensor",
                                         lambda e, m=m, k=k, wap=wap,
                                         pbx=ps.banks[bu[m]]: e.matmul(
                                             pbx[:, 0:T],
                                             wap[:, 128 * m:128 * (m + 1)], g1(k),
                                             start=(k == 0), stop=(k == 7)),
                                         waits=[wtok], inc=(m == 3))
                        wslot[k % 2] = ("et", p.cnt["et"])
                    for m in range(4):
                        iu = 4 * mug + m
                        t1 = p.run("scalar", lambda e, iu=iu, pbx=ps.banks[bu[m]]:
                                   e.activation(slab(G5, iu), pbx[:, 0:T], AF.Relu),
                                   waits=[utok, g5_rd if mug == 0 else None],
                                   inc=True)
                        t2 = p.run("scalar", lambda e, iu=iu: e.activation(
                            slab(G5, iu), slab(G5, iu), AF.Square))
                        up_tok[iu] = ("es", p.cnt["es"])
                        ps.release(bu[m], up_tok[iu])
                for mog in range(2):
                    bd = [ps.acquire(pool=(4, 5, 6, 7)) for _ in range(4)]
                    dtok = None
                    for k in range(16):
                        ku = 16 * half + k
                        wtok, wap = wload(2 + k % 2,
                                          self.down_t[128 * ku:128 * (ku + 1),
                                                      512 * mog:512 * (mog + 1)])
                        for m in range(4):
                            if k == 0:
                                p.wait("tensor", ps.free_tok[bd[m]])
                            dtok = p.run("tensor",
                                         lambda e, m=m, k=k, wap=wap,
                                         pbx=ps.banks[bd[m]]: e.matmul(
                                             pbx[:, 0:T],
                                             wap[:, 128 * m:128 * (m + 1)],
                                             slab(G5, k), start=(k == 0),
                                             stop=(k == 15)),
                                         waits=[wtok, up_tok[k]],
                                         inc=(m == 3))
                        wslot[2 + k % 2] = ("et", p.cnt["et"])
                    for m in range(4):
                        mo = 4 * mog + m
                        if half == 0:
                            td = p.run("scalar", lambda e, mo=mo,
                                       pbx=ps.banks[bd[m]]: e.activation(
                                           dacc(mo), pbx[:, 0:T], AF.Copy),
                                       waits=[dtok, dacc_rd[mo]], inc=True)
                            ps.release(bd[m], td)
                        else:
                            ta = SLT[:, (mo % 2) * T:(mo % 2 + 1) * T]
                            xr = SLT[:, (2 + mo % 2) * T:(3 + mo % 2) * T]
                            txr = p.dma("sync", xr,
                                        src[128 * mo:128 * (mo + 1), t0:t0 + T],
                                        waits=[rslot.get(("xr", mo % 2)),
                                               ("ev", p.cnt["ev"])] + srdy,
                                        stream="xr%d" % (mo % 2))
                            t1 = p.run("vector", lambda e, ta=ta, mo=mo,
                                       pbx=ps.banks[bd[m]]: e.tensor_add(
                                           ta, pbx[:, 0:T], dacc(mo)),
                                       waits=[dtok, rslot.get(("ta", mo % 2))],
                                       inc=True)
                            ps.release(bd[m], t1)
                            dacc_rd[mo] = t1
                            p.run("vector", lambda e, ta=ta, mo=mo: e.tensor_add(
                                ta, ta, gh(mo)), waits=[routed_rd[mo]])
                            routed_rd[mo] = ("ev", p.cnt["ev"])
                            tv = p.run("vector", lambda e, ta=ta, xr=xr:
                                       e.tensor_add(ta, ta, xr),
                                       waits=[txr], inc=True)
                            rslot[("xr", mo % 2)] = tv
                            tst = p.dma("gpsimd",
                                        dst[128 * mo:128 * (mo + 1), t0:t0 + T],
                                        ta, waits=[tv], stream="st%d" % (mo % 2))
                            rslot[("ta", mo % 2)] = tst
                            if c == NCH - 1 and is_last:
                                self.final_tokens.append(tst)
                g5_rd = ("et", p.cnt["et"])
            self.store_toks.append([("st0", p.cnt.get("st0", 0)),
                                    ("st1", p.cnt.get("st1", 0))])
            last_read_xn = ("et", p.cnt["et"])
            for g in range(8):
                xn_free[g] = last_read_xn
            g3_free = ("ev", p.cnt["ev"])
            xd_rd = [("et", p.cnt["et"])]


    def emit_attn(self, src, dst, is_last, src_ready=None):
        p, ps, nc = self.p, self.ps, self.nc
        G1, GH, G3, G4, G5 = self.G1, self.GH, self.G3, self.G4, self.G5
        SLT, XD, BCB, LNB, WB, HALO = (self.SLT, self.XD, self.BCB, self.LNB,
                                       self.WB, self.HALO)

        def g1(g):
            return G1[:, g * T:(g + 1) * T]

        def ghx(g):
            return GH[:, g * (T + 3):(g + 1) * (T + 3)]

        def slab(S, g):
            return S[:, g * T:(g + 1) * T]

        bnd = [p.now(k) for k in ("et", "ev", "es", "eg")]
        BWMX = self.ZST
        t_bw = p.dma("sync", BWMX[:, 0:64],
                     bass.AP(self.bw_.tensor, 0, [[8, 128], [1024, 8], [1, 8]]),
                     waits=[("zr0", p.cnt.get("zr0", 0)),
                            ("zr1", p.cnt.get("zr1", 0))] + bnd,
                     stream="bwx")
        t_bw = p.dma("sync", BWMX[:, 64:128],
                     bass.AP(self.mixw_.tensor, 0, [[8, 128], [1024, 8], [1, 8]]),
                     stream="bwx")
        xn_free = [bnd[0]] * 8
        g3_free = bnd[1]
        wslot = [None] * 4
        rslot = {("ta", 0): ("st0", p.cnt.get("st0", 0)),
                 ("ta", 1): ("st1", p.cnt.get("st1", 0)),
                 ("xr", 0): ("xr0", p.cnt.get("xr0", 0)),
                 ("xr", 1): ("xr1", p.cnt.get("xr1", 0)),
                 ("sq", 0): bnd[1], ("sq", 1): bnd[1],
                 ("kb", 0): ("st0", p.cnt.get("st0", 0)),
                 ("kb", 1): ("st1", p.cnt.get("st1", 0))}
        gh_free = [bnd[0]] * 8
        g3fr = [[] for _ in range(8)]
        halo_tok = [bnd[1]] * 24
        lnb_free = [bnd[1], ("web0", p.cnt.get("web0", 0))]

        def wload(si, src_ap, waits=()):
            dstp = WB[:, si * 512:(si + 1) * 512]
            tok = p.dma("sync", dstp, src_ap,
                        waits=[wslot[si]] + list(waits), stream="ws%db" % si)
            return tok, dstp

        def wcols4(wt, k0, m):
            return bass.AP(wt.tensor, 128 * k0 * 1024 + 128 * m,
                           [[1024, 128], [128 * 1024, 4], [1, 128]])

        beta_rows = None
        for c in range(NCH):
            t0 = c * T
            srdy = src_ready[c] if src_ready else []
            xtok = [p.dma("sync", g1(g), src[128 * g:128 * (g + 1), t0:t0 + T],
                          waits=[xn_free[g]] + srdy, stream="xl") for g in range(8)]
            xtok = [xtok[-1]] * 8
            xn_tok = self.emit_ln([g1(g) for g in range(8)], t0, xtok,
                                  lambda g: slab(G3, g), extra_read_tok=g3_free)

            for woff, bias, rowb, tagw in ((0, None, 58, "betw"),
                                           (64, self.MIXB, 66, "gw")):
                bb = ps.acquire()
                pbb = ps.banks[bb]
                tm = ps.group(bb, [
                    (lambda e, st, sp, k=k, pbb=pbb, woff=woff: e.matmul(
                        pbb[0:8, 0:T],
                        BWMX[:, woff + 8 * k:woff + 8 * (k + 1)], g1(k),
                        start=st, stop=sp)) for k in range(8)],
                    extra_waits=xn_tok + [self.ini_tok, self.mixb_tok, t_bw])
                te = p.run("scalar", lambda e, pbb=pbb, bias=bias: e.activation(
                    XD[0:8, 0:T], pbb[0:8, 0:T], AF.Sigmoid,
                    **({} if bias is None else {"bias": bias})),
                    waits=[tm, rslot.get("xd")], inc=True)
                ps.release(bb, te)
                tb = p.dma("gpsimd", bass.AP(self.bounce.tensor, rowb * L + t0,
                                             [[L, 8], [1, T]]), XD[0:8, 0:T],
                           waits=[te], stream=tagw)
                rslot["xd"] = tb
                if rowb == 58:
                    beta_rows = tb
                else:
                    self.g_rows_tok = tb

            for ti, (wt, scr) in enumerate(((self.qw_, self.q_s),
                                            (self.kw_, self.k_s),
                                            (self.vw_, self.v_s))):
                ghtok = [None] * 8
                for m in range(8):
                    bq = ps.acquire()
                    pbq = ps.banks[bq]
                    gtok = None
                    wap = None
                    for k in range(8):
                        if k % 4 == 0:
                            si = (2 * ti + k // 4) % 4
                            wtok, wap = wload(si, wcols4(wt, k, m))
                        if k == 0:
                            p.wait("tensor", ps.free_tok[bq])
                        gtok = p.run("tensor",
                                     lambda e, k=k, wap=wap, pbq=pbq: e.matmul(
                                         pbq[:, 0:T],
                                         wap[:, 128 * (k % 4):128 * (k % 4 + 1)],
                                         g1(k), start=(k == 0), stop=(k == 7)),
                                     waits=[wtok] + (xn_tok if k == 0 else []),
                                     inc=(k % 4 == 3))
                        if k % 4 == 3:
                            wslot[(2 * ti + k // 4) % 4] = ("et", p.cnt["et"])
                    hidx = ti * 8 + m
                    if c == 0:
                        th = p.run("vector", lambda e, m=m: e.memset(
                            ghx(m)[:, 0:3], 0.0), waits=[gh_free[m]])
                    else:
                        th = p.run("vector", lambda e, m=m, hidx=hidx: e.tensor_copy(
                            ghx(m)[:, 0:3], HALO[:, 3 * hidx:3 * hidx + 3]),
                            waits=[gh_free[m], halo_tok[hidx]])
                    ghtok[m] = p.run("scalar", lambda e, m=m, pbq=pbq: e.activation(
                        ghx(m)[:, 3:3 + T], pbq[:, 0:T], AF.Copy),
                        waits=[gtok, th], inc=True)
                    ps.release(bq, ghtok[m])
                conv_tok = [None] * 8
                for m in range(8):
                    hidx = ti * 8 + m
                    p.run("vector", lambda e, m=m, ti=ti: e.tensor_scalar(
                        slab(G3, 8 + m), ghx(m)[:, 0:T], self.CTAP(ti, m, 0), None,
                        ALU.mult),
                        waits=[ghtok[m], self.ini_tok, g3_free, bnd[0]] + g3fr[m])
                    tcv = None
                    for j in (1, 2, 3):
                        tcv = p.run("vector", lambda e, m=m, j=j, ti=ti:
                                    e.scalar_tensor_tensor(
                                        slab(G3, 8 + m), ghx(m)[:, j:j + T],
                                        self.CTAP(ti, m, j), slab(G3, 8 + m),
                                        ALU.mult, ALU.add))
                    halo_tok[hidx] = p.run("vector",
                                           lambda e, m=m, hidx=hidx: e.tensor_copy(
                                               HALO[:, 3 * hidx:3 * hidx + 3],
                                               ghx(m)[:, T:T + 3]), inc=True)
                    gh_free[m] = halo_tok[hidx]
                    conv_tok[m] = p.run("scalar", lambda e, m=m: e.activation(
                        slab(G3, 8 + m), slab(G3, 8 + m), AF.Silu),
                        waits=[tcv], inc=True)
                if ti < 2:
                    rowb = 42 if ti == 0 else 50
                    for m in range(8):
                        sqa = SLT[:, (4 + m % 2) * T:(5 + m % 2) * T]
                        tsq = p.run("scalar", lambda e, m=m, sqa=sqa: e.activation(
                            sqa, slab(G3, 8 + m), AF.Square),
                            waits=[conv_tok[m], rslot.get(("sq", m % 2))], inc=True)
                        bs = ps.acquire()
                        pbs = ps.banks[bs]
                        tms = ps.group(bs, [(lambda e, st, sp, sqa=sqa, pbs=pbs:
                                             e.matmul(pbs[0:1, 0:T], self.ONES, sqa,
                                                      start=st, stop=sp))],
                                       extra_waits=[tsq])
                        r1 = SLT[0:1, (4 + m % 2) * T:(5 + m % 2) * T]
                        tv1 = p.run("vector", lambda e, r1=r1, pbs=pbs:
                                    e.tensor_scalar_add(r1, pbs[0:1, 0:T], 1e-12),
                                    waits=[tms], inc=True)
                        ps.release(bs, tv1)
                        tv2 = p.run("scalar", lambda e, r1=r1: e.activation(
                            r1, r1, AF.Sqrt), waits=[tv1], inc=True)
                        tv3 = p.run("vector", lambda e, r1=r1: e.reciprocal(r1, r1),
                                    waits=[tv2], inc=True)
                        tbr = p.dma("gpsimd",
                                    bass.AP(self.bounce.tensor,
                                            (rowb + m) * L + t0, [[L, 1], [1, T]]),
                                    r1, waits=[tv3], stream="rnw%d" % ti)
                        rslot[("sq", m % 2)] = tbr
                        tbc = p.dma("gpsimd", LNB[:, (m % 2) * T:(m % 2 + 1) * T],
                                    self.brow(rowb + m, t0, T),
                                    waits=[tbr, lnb_free[m % 2]],
                                    stream="rnb%d" % (m % 2))
                        tsc = p.run("vector", lambda e, m=m: e.tensor_mul(
                            slab(G3, 8 + m), slab(G3, 8 + m),
                            LNB[:, (m % 2) * T:(m % 2 + 1) * T]),
                            waits=[tbc], inc=True)
                        lnb_free[m % 2] = tsc
                        conv_tok[m] = tsc
                tqs = None
                for m in range(8):
                    tqs = p.dma("gpsimd", scr[128 * m:128 * (m + 1), t0:t0 + T],
                                slab(G3, 8 + m), waits=[conv_tok[m]],
                                stream="qs%d" % ti)
                for m in range(8):
                    g3fr[m] = [tqs]
                if ti >= 1:
                    dscr = self.kb_s if ti == 1 else self.vb_s
                    for m in range(8):
                        tbc = p.dma("gpsimd", LNB[:, (m % 2) * T:(m % 2 + 1) * T],
                                    self.brow(58 + m, t0, T),
                                    waits=[beta_rows, lnb_free[m % 2]],
                                    stream="rnb%d" % (m % 2))
                        kslt = SLT[:, (m % 2) * T:(m % 2 + 1) * T]
                        tkb = p.run("vector", lambda e, m=m, kslt=kslt: e.tensor_mul(
                            kslt, slab(G3, 8 + m),
                            LNB[:, (m % 2) * T:(m % 2 + 1) * T]),
                            waits=[tbc, conv_tok[m], rslot.get(("kb", m % 2))],
                            inc=True)
                        g3fr[m].append(tkb)
                        lnb_free[m % 2] = tkb
                        tskb = p.dma("gpsimd",
                                     dscr[128 * m:128 * (m + 1), t0:t0 + T],
                                     kslt, waits=[tkb], stream="kbs%d" % (m % 2))
                        rslot[("kb", m % 2)] = tskb
            last_xn_rd = ("et", p.cnt["et"])
            for g in range(8):
                xn_free[g] = last_xn_rd
            g3_free = ("ev", p.cnt["ev"])

        qkv_all = [("qs0", p.cnt.get("qs0", 0)), ("qs1", p.cnt.get("qs1", 0)),
                   ("kbs0", p.cnt.get("kbs0", 0)), ("kbs1", p.cnt.get("kbs1", 0)),
                   ("qs2", p.cnt.get("qs2", 0))]

        if "B" not in ATTN_PHASES:
            return
        CC = 128
        NCC = (NCH * T) // CC
        t_s0 = p.run("vector", lambda e: e.memset(G4[:, 0:1024], 0.0),
                     waits=bnd + [("eg", p.cnt["eg"])], inc=True)
        S_tok = [t_s0] * 8

        def g5t(i, par):
            base = par * 16
            return G5[:, (base + i) * 128:(base + i + 1) * 128]

        slotB = {}
        for par in range(4):
            for ii in range(4):
                slotB[(ii, par)] = bnd[0]
            slotB[("z", par)] = bnd[1]
            slotB[("X", par)] = bnd[1]
            slotB[("Y", par)] = bnd[1]
            slotB[("nw", par)] = bnd[1]
            slotB[("u1", par)] = bnd[1]
            slotB[("at", par)] = bnd[1]
            slotB[("ob", par)] = bnd[1]
        def b_iter(ci, h):
            par = (ci * 8 + h) % 4
            bk_tgl = [0]

            def nb():
                i2 = 2 * par + bk_tgl[0]
                bk_tgl[0] ^= 1
                return i2
            qt, kt, kbt, vbt = (g5t(i, par) for i in range(4))
            zb = G5[:, (par * 16 + 4) * 128:(par * 16 + 7) * 128]
            Xc, Yc, Xn, Yn = (g5t(i, par) for i in range(7, 11))
            negw, U1, attnT, OB = (g5t(i, par) for i in range(11, 15))
            Sh = G4[:, 128 * h:128 * (h + 1)]
            ld = None
            for ii, scr in enumerate((self.q_s, self.k_s, self.kb_s,
                                      self.vb_s)):
                ld = p.dma("sync", g5t(ii, par),
                           scr[128 * h:128 * (h + 1), CC * ci:CC * (ci + 1)],
                           waits=[slotB[(ii, par)], qkv_all[ii] if ii < 4 else None],
                           stream="bl%d" % par)
            ldt = ld
            yield
            bM = nb()
            pbM = ps.banks[bM]
            p.wait("tensor", ps.free_tok[bM])
            p.run("tensor", lambda e, pbM=pbM, kbt=kbt, kt=kt: e.matmul(
                pbM[:, 0:128], kbt, kt, start=True, stop=True), waits=[ldt])
            tM = p.run("tensor", lambda e, pbM=pbM, kbt=kbt, kt=kt: e.matmul(
                pbM[:, 128:256], kt, kbt, start=True, stop=True), inc=True)
            yield
            tN = p.run("vector", lambda e, pbM=pbM, Xc=Xc: e.tensor_mul(
                Xc, pbM[:, 0:128], self.LMSK[:, :]),
                waits=[tM, slotB[("X", par)], self.ini_tok], inc=True)
            tY = p.run("vector", lambda e, pbM=pbM, Yc=Yc: e.tensor_mul(
                Yc, pbM[:, 128:256], self.UMSK[:, :]),
                waits=[slotB[("Y", par)]], inc=True)
            ps.release(bM, tY)
            bZ = nb()
            pbZ = ps.banks[bZ]
            p.wait("tensor", ps.free_tok[bZ])
            p.run("tensor", lambda e, pbZ=pbZ, vbt=vbt: e.transpose(
                pbZ[:, 0:128], vbt, self.IDT[:, :]), waits=[ldt])
            p.run("tensor", lambda e, pbZ=pbZ, kbt=kbt: e.transpose(
                pbZ[:, 128:256], kbt, self.IDT[:, :]))
            tz3 = p.run("tensor", lambda e, pbZ=pbZ, kt=kt: e.transpose(
                pbZ[:, 256:384], kt, self.IDT[:, :]), inc=True)
            yield
            zt = p.run("vector", lambda e, zb=zb, pbZ=pbZ: e.tensor_copy(
                zb, pbZ[:, 0:384]),
                waits=[tz3, slotB[("z", par)]], inc=True)
            ps.release(bZ, zt)
            xk, yk = Xc, Yc
            txk, tyk = tN, tY
            for kst in range(7):
                bA = nb()
                pbA = ps.banks[bA]
                p.wait("tensor", ps.free_tok[bA])
                ta = p.run("tensor", lambda e, pbA=pbA, yk=yk, zb=zb: e.matmul(
                    pbA[:, 0:256], yk, zb[:, 0:256], start=True, stop=True),
                    waits=[tyk, zt], inc=True)
                if kst < 6:
                    bS = nb()
                    pbS = ps.banks[bS]
                    p.wait("tensor", ps.free_tok[bS])
                    p.run("tensor", lambda e, pbS=pbS, yk=yk, xk=xk: e.matmul(
                        pbS[:, 0:128], yk, xk, start=True, stop=True),
                        waits=[txk, tyk])
                    ts = p.run("tensor", lambda e, pbS=pbS, yk=yk, xk=xk:
                               e.matmul(pbS[:, 128:256], xk, yk,
                                        start=True, stop=True), inc=True)
                yield
                zt = p.run("vector", lambda e, zb=zb, pbA=pbA: e.tensor_add(
                    zb[:, 0:256], pbA[:, 0:256], zb[:, 0:256]),
                    waits=[ta], inc=True)
                ps.release(bA, zt)
                if kst < 6:
                    nxk = Xn if (kst % 2 == 0) else Xc
                    nyk = Yn if (kst % 2 == 0) else Yc
                    txk = p.run("vector", lambda e, nxk=nxk, pbS=pbS:
                                e.tensor_copy(nxk, pbS[:, 0:128]),
                                waits=[ts], inc=True)
                    tyk = p.run("scalar", lambda e, nyk=nyk, pbS=pbS:
                                e.activation(nyk, pbS[:, 128:256], AF.Copy),
                                waits=[ts, txk], inc=True)
                    ps.release(bS, tyk)
                    xk, yk = nxk, nyk
            slotB[("X", par)] = ("et", p.cnt["et"])
            slotB[("Y", par)] = ("et", p.cnt["et"])
            bW2 = nb()
            pbW2 = ps.banks[bW2]
            p.wait("tensor", ps.free_tok[bW2])
            tw1 = p.run("tensor", lambda e, pbW2=pbW2, zb=zb: e.transpose(
                pbW2[:, 0:128], zb[:, 128:256], self.IDT[:, :]),
                waits=[zt], inc=True)
            yield
            tw2 = p.run("vector", lambda e, negw=negw, pbW2=pbW2:
                        e.tensor_scalar_mul(negw, pbW2[:, 0:128], -1.0),
                        waits=[tw1, slotB[("nw", par)]], inc=True)
            ps.release(bW2, tw2)
            bU = nb()
            pbU = ps.banks[bU]
            p.wait("tensor", ps.free_tok[bU])
            p.run("tensor", lambda e, pbU=pbU, negw=negw, Sh=Sh: e.matmul(
                pbU[:, 0:128], negw, Sh, start=True, stop=False),
                waits=[tw2, S_tok[h]])
            tu = p.run("tensor", lambda e, pbU=pbU, zb=zb: e.matmul(
                pbU[:, 0:128], self.IDT[:, :], zb[:, 0:128],
                start=False, stop=True), inc=True)
            yield
            tu1 = p.run("scalar", lambda e, U1=U1, pbU=pbU: e.activation(
                U1, pbU[:, 0:128], AF.Copy),
                waits=[tu, slotB[("u1", par)]], inc=True)
            ps.release(bU, tu1)
            bAT = nb()
            pbAT = ps.banks[bAT]
            p.wait("tensor", ps.free_tok[bAT])
            tat = p.run("tensor", lambda e, pbAT=pbAT, kt=kt, qt=qt: e.matmul(
                pbAT[:, 0:128], kt, qt, start=True, stop=True),
                waits=[ldt], inc=True)
            yield
            tat2 = p.run("vector", lambda e, attnT=attnT, pbAT=pbAT:
                         e.tensor_mul(attnT, pbAT[:, 0:128], self.UIMSK[:, :]),
                         waits=[tat, slotB[("at", par)]], inc=True)
            ps.release(bAT, tat2)
            bO = nb()
            pbO = ps.banks[bO]
            p.wait("tensor", ps.free_tok[bO])
            p.run("tensor", lambda e, pbO=pbO, qt=qt, Sh=Sh: e.matmul(
                pbO[:, 0:128], Sh, qt, start=True, stop=False),
                waits=[S_tok[h], tu1])
            to = p.run("tensor", lambda e, pbO=pbO, U1=U1, attnT=attnT:
                       e.matmul(pbO[:, 0:128], U1, attnT,
                                start=False, stop=True),
                       waits=[tat2], inc=True)
            yield
            tob = p.run("scalar", lambda e, OB=OB, pbO=pbO: e.activation(
                OB, pbO[:, 0:128], AF.Copy),
                waits=[to, slotB[("ob", par)]], inc=True)
            ps.release(bO, tob)
            tos = p.dma("gpsimd", self.o_s[128 * h:128 * (h + 1),
                                           CC * ci:CC * (ci + 1)], OB,
                        waits=[tob], stream="os%d" % par)
            slotB[("ob", par)] = tos
            bSU = nb()
            pbSU = ps.banks[bSU]
            p.wait("tensor", ps.free_tok[bSU])
            tsu = p.run("tensor", lambda e, pbSU=pbSU, zb=zb, U1=U1: e.matmul(
                pbSU[:, 0:128], zb[:, 256:384], U1, start=True, stop=True),
                waits=[tu1], inc=True)
            yield
            tsa = p.run("vector", lambda e, pbSU=pbSU, Sh=Sh: e.tensor_add(
                Sh, pbSU[:, 0:128], Sh), waits=[tsu, to], inc=True)
            ps.release(bSU, tsa)
            S_tok[h] = tsa
            etn = ("et", p.cnt["et"])
            for ii in range(4):
                slotB[(ii, par)] = etn
            slotB[("z", par)] = tsu
            slotB[("nw", par)] = etn
            slotB[("u1", par)] = tsu
            slotB[("at", par)] = to

        n_it = min(NCC * 8, B_LIMIT)
        for blk in range(0, n_it, 4):
            gens = [b_iter((blk + j) // 8, (blk + j) % 8)
                    for j in range(min(4, n_it - blk))]
            live = list(gens)
            while live:
                nxt = []
                for gme in live:
                    try:
                        next(gme)
                        nxt.append(gme)
                    except StopIteration:
                        pass
                live = nxt
        o_all = [("os%d" % j, p.cnt.get("os%d" % j, 0)) for j in range(4)]

        if "C" not in ATTN_PHASES:
            return
        vt_all = ("qs2", p.cnt.get("qs2", 0))
        rms_free = [("rnb0", p.cnt.get("rnb0", 0)), ("rnb1", p.cnt.get("rnb1", 0))]
        bcb_guard = [bnd[1], ("wxw", p.cnt.get("wxw", 0)),
                     ("bcb", p.cnt.get("bcb", 0))]
        g3v_free = ("ev", p.cnt["ev"])
        for c in range(NCH):
            t0 = c * T
            otok = [p.dma("sync", g1(g), self.o_s[128 * g:128 * (g + 1), t0:t0 + T],
                          waits=o_all + [xn_free[g]], stream="ol")
                    for g in range(8)]
            otok = [otok[-1]] * 8
            vtok = [p.dma("sync", slab(G3, 8 + g),
                          self.v_s[128 * g:128 * (g + 1), t0:t0 + T],
                          waits=[vt_all, g3v_free], stream="vl") for g in range(8)]
            vtok = [vtok[-1]] * 8
            go_tok = []
            for m in range(8):
                tbc = p.dma("gpsimd", LNB[:, (m % 2) * T:(m % 2 + 1) * T],
                            self.brow(66 + m, t0, T),
                            waits=[self.g_rows_tok, lnb_free[m % 2]],
                            stream="rnb%d" % (m % 2))
                p.run("vector", lambda e, m=m: e.tensor_sub(
                    g1(m), g1(m), slab(G3, 8 + m)), waits=[otok[m], vtok[m]])
                p.run("vector", lambda e, m=m: e.tensor_mul(
                    g1(m), g1(m), LNB[:, (m % 2) * T:(m % 2 + 1) * T]),
                    waits=[tbc])
                t2 = p.run("vector", lambda e, m=m: e.tensor_add(
                    g1(m), g1(m), slab(G3, 8 + m)), inc=True)
                lnb_free[m % 2] = t2
                sqa = SLT[:, (4 + m % 2) * T:(5 + m % 2) * T]
                tsq = p.run("scalar", lambda e, m=m, sqa=sqa: e.activation(
                    sqa, g1(m), AF.Square),
                    waits=[t2, rslot.get(("sq", m % 2))], inc=True)
                bs = ps.acquire()
                pbs = ps.banks[bs]
                tms = ps.group(bs, [(lambda e, st, sp, sqa=sqa, pbs=pbs: e.matmul(
                    pbs[0:1, 0:T], self.ONES, sqa, start=st, stop=sp))],
                    extra_waits=[tsq])
                r1 = SLT[0:1, (4 + m % 2) * T:(5 + m % 2) * T]
                tv1 = p.run("vector", lambda e, r1=r1, pbs=pbs: e.tensor_scalar(
                    r1, pbs[0:1, 0:T], 1.0 / 128.0, LNEPS, ALU.mult, ALU.add),
                    waits=[tms], inc=True)
                ps.release(bs, tv1)
                tv2 = p.run("scalar", lambda e, r1=r1: e.activation(
                    r1, r1, AF.Sqrt), waits=[tv1], inc=True)
                tv3 = p.run("vector", lambda e, r1=r1: e.reciprocal(r1, r1),
                            waits=[tv2], inc=True)
                tbr = p.dma("gpsimd",
                            bass.AP(self.bounce.tensor, (42 + m) * L + t0,
                                    [[L, 1], [1, T]]), r1,
                            waits=[tv3, ("rnb0", p.cnt.get("rnb0", 0)) if c == 0
                                   else None], stream="rmw")
                rslot[("sq", m % 2)] = tbr
                tbc2 = p.dma("gpsimd", BCB[:, (m % 2) * T:(m % 2 + 1) * T],
                             self.brow(42 + m, t0, T),
                             waits=[tbr, rms_free[m % 2]] + bcb_guard,
                             stream="rmb%d" % (m % 2))
                gt = p.run("vector", lambda e, m=m: e.scalar_tensor_tensor(
                    g1(m), g1(m), self.WNORM, BCB[:, (m % 2) * T:(m % 2 + 1) * T],
                    ALU.mult, ALU.mult), waits=[tbc2, self.ini_tok], inc=True)
                rms_free[m % 2] = gt
                go_tok.append(gt)
            srdy = src_ready[c] if src_ready else []
            for mo in range(8):
                bq = ps.acquire()
                pbq = ps.banks[bq]
                gtok = None
                wap = None
                for k in range(8):
                    if k % 4 == 0:
                        si = (2 * mo + k // 4) % 4
                        wtok, wap = wload(si, wcols4(self.ow_, k, mo))
                    if k == 0:
                        p.wait("tensor", ps.free_tok[bq])
                    gtok = p.run("tensor", lambda e, k=k, wap=wap, pbq=pbq: e.matmul(
                        pbq[:, 0:T], wap[:, 128 * (k % 4):128 * (k % 4 + 1)],
                        g1(k), start=(k == 0), stop=(k == 7)),
                        waits=[wtok] + (go_tok if k == 0 else []),
                        inc=(k % 4 == 3))
                    if k % 4 == 3:
                        wslot[(2 * mo + k // 4) % 4] = ("et", p.cnt["et"])
                ta = SLT[:, (mo % 2) * T:(mo % 2 + 1) * T]
                xr = SLT[:, (2 + mo % 2) * T:(3 + mo % 2) * T]
                txr = p.dma("sync", xr, src[128 * mo:128 * (mo + 1), t0:t0 + T],
                            waits=[rslot.get(("xr", mo % 2))] + srdy,
                            stream="xr%d" % (mo % 2))
                te = p.run("scalar", lambda e, pbq=pbq, ta=ta: e.activation(
                    ta, pbq[:, 0:T], AF.Copy),
                    waits=[gtok, rslot.get(("ta", mo % 2)),
                           rslot.get(("kb", mo % 2))], inc=True)
                ps.release(bq, te)
                tv = p.run("vector", lambda e, ta=ta, xr=xr: e.tensor_add(
                    ta, ta, xr), waits=[te, txr], inc=True)
                rslot[("xr", mo % 2)] = tv
                tst = p.dma("gpsimd", dst[128 * mo:128 * (mo + 1), t0:t0 + T],
                            ta, waits=[tv], stream="st%d" % (mo % 2))
                rslot[("ta", mo % 2)] = tst
                if c == NCH - 1 and is_last:
                    self.final_tokens.append(tst)
            etc = ("et", p.cnt["et"])
            for g in range(8):
                xn_free[g] = etc
            g3v_free = ("ev", p.cnt["ev"])
            self.store_toks.append([("st0", p.cnt.get("st0", 0)),
                                    ("st1", p.cnt.get("st1", 0))])

    def _build(self):
        nc, p = self.nc, self.p
        din, sbuf = self.din, self.sbuf

        self.x_in = din("x_in", [D, L])
        self.w_in_t = din("w_in_t", [D, 2 * DIN])
        b_in_c = din("b_in_c", [128, 32])
        conv_w_r = din("conv_w_r", [128, 64])
        conv_b_r = din("conv_b_r", [128, 16])
        w_x_t = din("w_x_t", [DIN, 96])
        b_x_c = din("b_x_c", [96, 1])
        w_dt_t = din("w_dt_t", [DTR, DIN])
        b_dt_r = din("b_dt_r", [128, 16])
        a_cols = din("a_cols", [128, 256])
        d_par_r = din("d_par_r", [128, 16])
        self.w_out_t = din("w_out_t", [DIN, D])
        b_out_c = din("b_out_c", [128, 8])
        ones_c = din("ones_c", [128, 1])
        gate_t = din("gate_t", [D, 8])
        corr_r = din("corr_r", [1, 8])
        self.fc1_t = din("fc1_t", [D, 512])
        self.w1f = din("w1f", [8 * 512, 512])
        self.w2f = din("w2f", [8 * 512, 512])
        self.fc2_t = din("fc2_t", [512, D])
        self.up_t = din("up_t", [D, 4096])
        self.down_t = din("down_t", [4096, D])
        eye_c = din("eye_c", [128, 128])
        self.qw_ = din("qw_", [D, D])
        self.kw_ = din("kw_", [D, D])
        self.vw_ = din("vw_", [D, D])
        self.ow_ = din("ow_", [D, D])
        self.bw_ = din("bw_", [D, 8])
        self.mixw_ = din("mixw_", [D, 8])
        mixb_c = din("mixb_c", [8, 1])
        mixbias_c = din("mixbias_c", [8, 1])
        ctaps_c = din("ctaps_c", [128, 96])
        wnorm_c = din("wnorm_c", [128, 1])
        lmask_c = din("lmask_c", [128, 128])
        umask_c = din("umask_c", [128, 128])
        uimask_c = din("uimask_c", [128, 128])
        self.x_out = nc.dram_tensor("x_out", [D, L], F32, kind="ExternalOutput").ap()
        self.xres = nc.dram_tensor("xres", [D, L], F32, kind="Internal").ap()
        self.q_s = nc.dram_tensor("q_s", [D, L], F32, kind="Internal").ap()
        self.k_s = nc.dram_tensor("k_s", [D, L], F32, kind="Internal").ap()
        self.kb_s = nc.dram_tensor("kb_s", [D, L], F32, kind="Internal").ap()
        self.vb_s = nc.dram_tensor("vb_s", [D, L], F32, kind="Internal").ap()
        self.v_s = nc.dram_tensor("v_s", [D, L], F32, kind="Internal").ap()
        self.o_s = nc.dram_tensor("o_s", [D, L], F32, kind="Internal").ap()
        self.bounce = nc.dram_tensor("bounce", [80, L], F32, kind="Internal").ap()
        self.zsilu = nc.dram_tensor("zsilu", [DIN, L], F32, kind="Internal").ap()

        self.G1 = sbuf("G1", 8 * T)
        self.GH = sbuf("GH", 16 * (T + 3))
        self.G3 = sbuf("G3", 16 * T)
        self.G4 = sbuf("G4", 16 * T)
        self.G5 = sbuf("G5", 16 * T)
        self.WB = sbuf("WB", 2 * 1024)
        self.XD = sbuf("XD", T)
        self.SLT = sbuf("SLT", 8 * T)
        self.ZST = sbuf("ZST", 2 * T)
        self.LNB = sbuf("LNB", 2 * T)
        self.BCB = sbuf("BCB", 2 * T)
        self.ROWQ = sbuf("ROWQ", 3 * T, rows=1)
        self.HC = sbuf("HC", 256)
        self.HALO = sbuf("HALO", 72)
        self.CON = sbuf("CON", 432)
        self.WXT = sbuf("WXT", 16 * 96)
        self.WDT = sbuf("WDT", 16 * 128)
        self.GATE = sbuf("GATE", 64)
        self.IDT = sbuf("IDT", 128)
        self.LMSK = sbuf("LMSK", 128)
        self.UMSK = sbuf("UMSK", 128)
        self.UIMSK = sbuf("UIMSK", 128)
        self.ATC = sbuf("ATC", 100)

        CON = self.CON
        self.ONES = CON[:, 0:1]
        self.ACOL = lambda g, s: CON[:, 1 + 16 * g + s: 2 + 16 * g + s]
        self.DCOL = lambda g: CON[:, 257 + g: 258 + g]
        self.TAP = lambda g, j: CON[:, 273 + 4 * g + j: 274 + 4 * g + j]
        self.CBCOL = lambda g: CON[:, 337 + g: 338 + g]
        self.BDTCOL = lambda g: CON[:, 353 + g: 354 + g]
        self.BINCOL = lambda m: CON[:, 369 + m: 370 + m]
        self.BXCOL = CON[:96, 401:402]
        self.BOUTCOL = lambda m: CON[:, 402 + m: 403 + m]

        def brow(r, t0, n, np_=128):
            return bass.AP(self.bounce.tensor, r * L + t0, [[0, np_], [1, n]])
        self.brow = brow

        p.dma("sync", self.ONES, ones_c[:, :], stream="ini")
        p.dma("sync", self.SLT[:, 0:256], a_cols[:, :], stream="ini")
        p.dma("sync", CON[:, 257:273], d_par_r[:, :], stream="ini")
        p.dma("sync", CON[:, 273:337], conv_w_r[:, :], stream="ini")
        p.dma("sync", CON[:, 337:353], conv_b_r[:, :], stream="ini")
        p.dma("sync", CON[:, 353:369], b_dt_r[:, :], stream="ini")
        p.dma("sync", CON[:, 369:401], b_in_c[:, :], stream="ini")
        p.dma("sync", self.BXCOL, b_x_c[:, :], stream="ini")
        p.dma("sync", CON[:, 402:410], b_out_c[:, :], stream="ini")
        p.dma("sync", self.WXT[:, :],
              bass.AP(w_x_t.tensor, 0, [[96, 128], [128 * 96, 16], [1, 96]]),
              stream="ini")
        p.dma("sync", self.WDT[0:64, :],
              bass.AP(w_dt_t.tensor, 0, [[DIN, 64], [1, 16 * 128]]),
              stream="ini")
        p.dma("sync", self.GATE[:, :],
              bass.AP(gate_t.tensor, 0, [[8, 128], [1024, 8], [1, 8]]),
              stream="ini")
        p.dma("sync", self.IDT[:, :], eye_c[:, :], stream="ini")
        p.dma("sync", CON[:, 410:418],
              bass.AP(corr_r.tensor, 0, [[0, 128], [1, 8]]), stream="ini")
        p.dma("sync", self.LMSK[:, :], lmask_c[:, :], stream="ini")
        p.dma("sync", self.UMSK[:, :], umask_c[:, :], stream="ini")
        p.dma("sync", self.UIMSK[:, :], uimask_c[:, :], stream="ini")
        p.dma("sync", self.ATC[0:8, 0:1], mixb_c[:, :], stream="ini")
        p.dma("sync", self.ATC[0:8, 1:2], mixbias_c[:, :], stream="ini")
        p.dma("sync", self.ATC[:, 2:3], wnorm_c[:, :], stream="ini")
        ini = p.dma("sync", self.ATC[:, 4:100], ctaps_c[:, :], stream="ini")
        self.MIXB = self.ATC[0:8, 0:1]
        self.WNORM = self.ATC[:, 2:3]
        self.CTAP = lambda ti, m, j: self.ATC[:, 4 + 32 * ti + 4 * m + j:
                                              5 + 32 * ti + 4 * m + j]
        self.mixb_tok = p.run("vector", lambda e: e.tensor_add(
            self.ATC[0:8, 0:1], self.ATC[0:8, 0:1], self.ATC[0:8, 1:2]),
            waits=[ini], inc=True)
        self.ini_tok = ini
        self.wxt_tok = ini
        self.wdt_tok = ini
        self.CORR = CON[:, 410:418]
        self.ONES8 = CON[:, 418:426]
        p.run("vector", lambda e: e.memset(CON[:, 418:426], 1.0))
        t_e = p.run("scalar", lambda e: e.activation(CON[:, 1:257], self.SLT[:, 0:256],
                                                     AF.Exp), waits=[ini], inc=True)
        self.t_acol = p.run("vector", lambda e: e.tensor_scalar_mul(
            CON[:, 1:257], CON[:, 1:257], -1.0), waits=[t_e], inc=True)
        t_hc = p.run("vector", lambda e: e.memset(self.HC[:, :], 0.0), inc=True)
        self.t_zero = t_hc
        self.t_g3w = t_hc
        self.bcb_free = [None, None]
        self.prev_tb = (None, None)

        self.store_toks = []
        dst1 = self.x_out if self.nsub == 1 else self.xres
        self.emit_mamba(self.x_in, dst1)
        if self.nsub >= 2:
            prev = self.store_toks
            self.store_toks = []
            dst2 = self.x_out if self.nsub == 2 else self.xres
            self.emit_ffn(self.xres, dst2, is_last=(self.nsub == 2), tag="a",
                          src_ready=prev)
        if self.nsub >= 3:
            prev = self.store_toks
            self.store_toks = []
            dst3 = self.x_out if self.nsub == 3 else self.xres
            self.emit_attn(self.xres, dst3, is_last=(self.nsub == 3),
                           src_ready=prev)
        if self.nsub >= 4:
            prev = self.store_toks
            self.store_toks = []
            self.emit_ffn(self.xres, self.x_out, is_last=True, tag="b",
                          src_ready=prev)

        self.p.finalize(self.final_tokens)


_PROG_CACHE = {}


def _get_prog(nsub):
    if nsub not in _PROG_CACHE:
        _PROG_CACHE[nsub] = Prog(nsub)
    return _PROG_CACHE[nsub]


def _host_layout(inputs):
    x = np.ascontiguousarray(inputs["L0_attn_a_x"])
    w = {}
    w["w_in_t"] = np.ascontiguousarray(inputs["L0_attn_a_W_in"].T)
    w["b_in_c"] = np.ascontiguousarray(inputs["L0_attn_a_b_in"].reshape(32, 128).T)
    w["conv_w_r"] = np.ascontiguousarray(
        inputs["L0_attn_a_conv_w"].reshape(DIN, 4).reshape(16, 128, 4)
        .transpose(1, 0, 2).reshape(128, 64))
    w["conv_b_r"] = np.ascontiguousarray(inputs["L0_attn_a_conv_b"].reshape(16, 128).T)
    w["w_x_t"] = np.ascontiguousarray(inputs["L0_attn_a_W_x"].T)
    w["b_x_c"] = np.ascontiguousarray(inputs["L0_attn_a_b_x"].reshape(96, 1))
    w["w_dt_t"] = np.ascontiguousarray(inputs["L0_attn_a_W_dt"].T)
    w["b_dt_r"] = np.ascontiguousarray(inputs["L0_attn_a_b_dt"].reshape(16, 128).T)
    w["a_cols"] = np.ascontiguousarray(
        inputs["L0_attn_a_A_log"].reshape(16, 128, 16).transpose(1, 0, 2).reshape(128, 256))
    w["d_par_r"] = np.ascontiguousarray(inputs["L0_attn_a_D_param"].reshape(16, 128).T)
    w["w_out_t"] = np.ascontiguousarray(inputs["L0_attn_a_W_out"].T)
    w["b_out_c"] = np.ascontiguousarray(inputs["L0_attn_a_b_out"].reshape(8, 128).T)
    w["ones_c"] = np.ones((128, 1), np.float32)
    w["gate_t"] = np.ascontiguousarray(inputs["L0_ffn_gate"].T)
    w["corr_r"] = np.ascontiguousarray(inputs["L0_ffn_corr_bias"].reshape(1, 8))
    w["fc1_t"] = np.ascontiguousarray(inputs["L0_ffn_fc1_w"].T)
    w["w1f"] = np.ascontiguousarray(
        inputs["L0_ffn_w1"].transpose(0, 2, 1).reshape(8 * 512, 512))
    w["w2f"] = np.ascontiguousarray(
        inputs["L0_ffn_w2"].transpose(0, 2, 1).reshape(8 * 512, 512))
    w["fc2_t"] = np.ascontiguousarray(inputs["L0_ffn_fc2_w"].T)
    w["up_t"] = np.ascontiguousarray(inputs["L0_ffn_up_w"].T)
    w["down_t"] = np.ascontiguousarray(inputs["L0_ffn_down_w"].T)
    w["eye_c"] = np.eye(128, dtype=np.float32)
    w["qw_"] = np.ascontiguousarray(inputs["L0_attn_b_q_w"])
    w["kw_"] = np.ascontiguousarray(inputs["L0_attn_b_k_w"])
    w["vw_"] = np.ascontiguousarray(inputs["L0_attn_b_v_w"])
    w["ow_"] = np.ascontiguousarray(inputs["L0_attn_b_o_w"])
    w["bw_"] = np.ascontiguousarray(inputs["L0_attn_b_b_w"])
    w["mixw_"] = np.ascontiguousarray(inputs["L0_attn_b_mix_w"])
    w["mixb_c"] = np.ascontiguousarray(inputs["L0_attn_b_mix_b"].reshape(8, 1))
    w["mixbias_c"] = np.ascontiguousarray(inputs["L0_attn_b_mix_bias"].reshape(8, 1))
    taps = []
    for kk in ("L0_attn_b_conv_q_w", "L0_attn_b_conv_k_w", "L0_attn_b_conv_v_w"):
        taps.append(inputs[kk].reshape(8, 128, 4).transpose(1, 0, 2).reshape(128, 32))
    w["ctaps_c"] = np.ascontiguousarray(np.concatenate(taps, axis=1))
    w["wnorm_c"] = np.ascontiguousarray(inputs["L0_attn_b_o_norm_w"].reshape(128, 1))
    on = np.ones((128, 128), np.float32)
    w["lmask_c"] = np.ascontiguousarray(-np.tril(on, -1))
    w["umask_c"] = np.ascontiguousarray(-np.triu(on, 1))
    w["uimask_c"] = np.ascontiguousarray(np.triu(on, 0))
    maps = []
    for b in range(8):
        m = {"x_in": np.ascontiguousarray(x[b].T)}
        m.update(w)
        maps.append(m)
    return maps


def kernel(**inputs):
    prog = _get_prog(N_SUBLAYERS)
    maps = _host_layout(inputs)
    res = run_bass_kernel_spmd(prog.nc, maps, core_ids=list(range(8)))
    out = np.stack([np.asarray(r["x_out"]).T for r in res.results])
    return out.astype(np.float32)


# version 17 (final)
# speedup vs baseline: 1.0263x; 1.0263x over previous
import contextlib
import numpy as np
import concourse.bass as bass
import concourse.mybir as mybir
from concourse.bass_utils import run_bass_kernel_spmd

F32 = mybir.dt.float32
AF = mybir.ActivationFunctionType
ALU = mybir.AluOpType

D = 1024
L = 4096
DIN = 2048
DTR = 64
T = 512
NCH = L // T
LNEPS = 1e-5

ENGS = ("sync", "tensor", "vector", "scalar", "gpsimd")
QSEM = {"sync": "qs", "gpsimd": "qg", "scalar": "qc", "vector": "qv"}
ESEM = {"tensor": "et", "vector": "ev", "scalar": "es", "gpsimd": "eg"}

N_SUBLAYERS = 4
ATTN_PHASES = "ABC"
B_LIMIT = 10 ** 9


class Plan:
    def __init__(self, nc):
        self.nc = nc
        self.ops = {e: [] for e in ENGS}
        self.cnt = {}
        self.sems = {}
        self.stack = contextlib.ExitStack()
        self.last_wait = {e: {} for e in ENGS}
        for s in set(QSEM.values()) | set(ESEM.values()):
            self.sems[s] = self.stack.enter_context(nc.semaphore(s))
            self.cnt[s] = 0

    def wait(self, eng, tok):
        if tok is None:
            return
        s, v = tok
        if v <= 0:
            return
        lw = self.last_wait[eng]
        if lw.get(s, -1) >= v:
            return
        lw[s] = v
        sem = self.sems[s]
        self.ops[eng].append(lambda e, sem=sem, v=v: e.wait_ge(sem, v))

    def now(self, eng_or_sem):
        s = ESEM.get(eng_or_sem) or QSEM.get(eng_or_sem) or eng_or_sem
        return (s, self.cnt[s])

    def stream_sem(self, name):
        if name not in self.sems:
            self.sems[name] = self.stack.enter_context(self.nc.semaphore(name))
            self.cnt[name] = 0
        return self.sems[name]

    def dma(self, q, out, in_, accum=None, waits=(), stream=None):
        qs = stream if stream is not None else QSEM[q]
        sem = self.stream_sem(qs)
        if q != "sync":
            es = ESEM.get(q)
            if es:
                self.wait(q, (es, self.cnt[es]))
        for t in waits:
            self.wait(q, t)
        self.cnt[qs] += 16
        v = self.cnt[qs]
        if accum is None:
            self.ops[q].append(lambda e, o=out, i=in_, sem=sem:
                               e.dma_start(out=o, in_=i).then_inc(sem, 16))
        else:
            self.ops[q].append(lambda e, o=out, i=in_, a=accum, sem=sem:
                               e.dma_start(out=o, in_=i, accum_op=a).then_inc(sem, 16))
        return (qs, v)

    def run(self, eng, fn, waits=(), inc=False):
        es = ESEM[eng]
        if eng != "tensor":
            self.wait(eng, (es, self.cnt[es]))
            inc = True
        for t in waits:
            self.wait(eng, t)
        if inc:
            self.cnt[es] += 1
            sem = self.sems[es]
            self.ops[eng].append(lambda e, fn=fn, sem=sem: fn(e).then_inc(sem, 1))
            return (es, self.cnt[es])
        self.ops[eng].append(lambda e, fn=fn: fn(e))
        return None

    def finalize(self, final_tokens):
        nc, plan = self.nc, self

        with nc.Block() as block:
            @block.sync
            def _(e):
                for fn in plan.ops["sync"]:
                    fn(e)
                for s, v in final_tokens:
                    e.wait_ge(plan.sems[s], v)

            @block.tensor
            def _(e):
                for fn in plan.ops["tensor"]:
                    fn(e)

            @block.vector
            def _(e):
                for fn in plan.ops["vector"]:
                    fn(e)

            @block.scalar
            def _(e):
                for fn in plan.ops["scalar"]:
                    fn(e)

            @block.gpsimd
            def _(e):
                for fn in plan.ops["gpsimd"]:
                    fn(e)


class Psum:
    def __init__(self, nc, plan):
        self.p = plan
        self.banks = [plan.stack.enter_context(nc.psum_tensor(f"pb{i}", [128, 512], F32))
                      for i in range(8)]
        self.free_tok = [None] * 8
        self.rr = 0

    def acquire(self, pool=None):
        while True:
            i = self.rr
            self.rr = (self.rr + 1) % 8
            if pool is None or i in pool:
                return i

    def group(self, idx, mms, extra_waits=()):
        p = self.p
        p.wait("tensor", self.free_tok[idx])
        for t in extra_waits:
            p.wait("tensor", t)
        tok, n = None, len(mms)
        for j, fn in enumerate(mms):
            last = j == n - 1
            tok = p.run("tensor", (lambda e, fn=fn, st=(j == 0), sp=last: fn(e, st, sp)),
                        inc=last)
        return tok

    def release(self, idx, tok):
        self.free_tok[idx] = tok


class Prog:
    def __init__(self, nsub=N_SUBLAYERS):
        nc = bass.Bass("TRN2", target_bir_lowering=False)
        self.nc = nc
        self.p = Plan(nc)
        self.ps = Psum(nc, self.p)
        self.nsub = nsub
        self.final_tokens = []
        self._build()

    def din(self, name, shape):
        return self.nc.dram_tensor(name, shape, F32, kind="ExternalInput").ap()

    def sbuf(self, name, cols, rows=128):
        return self.p.stack.enter_context(self.nc.sbuf_tensor(name, [rows, cols], F32))

    def emit_ln(self, tiles, t0, xtok, sq_area, extra_read_tok=None):
        p, ps = self.p, self.ps
        t_sqall = None
        for g in range(8):
            t_sqall = p.run("scalar",
                            lambda e, g=g: e.activation(sq_area(g), tiles[g], AF.Square),
                            waits=[xtok[g]] + ([extra_read_tok] if g == 0 else []),
                            inc=(g == 7))
        b0 = ps.acquire()
        pb0 = self.ps.banks[b0]
        t_sx = ps.group(b0, [
            (lambda e, st, sp, g=g: e.matmul(pb0[0:1, 0:T], self.ONES, tiles[g],
                                             start=st, stop=sp)) for g in range(8)],
            extra_waits=xtok)
        b1 = ps.acquire()
        pb1 = self.ps.banks[b1]
        t_sq = ps.group(b1, [
            (lambda e, st, sp, g=g: e.matmul(pb1[0:1, 0:T], self.ONES, sq_area(g),
                                             start=st, stop=sp)) for g in range(8)],
            extra_waits=[t_sqall])
        R = self.ROWQ
        mu, va, ex = (R[0:1, i * T:(i + 1) * T] for i in range(3))
        t1 = p.run("vector", lambda e: e.tensor_scalar_mul(mu, pb0[0:1, 0:T], 1.0 / D),
                   waits=[t_sx], inc=True)
        ps.release(b0, t1)
        t2 = p.run("vector", lambda e: e.tensor_scalar_mul(va, pb1[0:1, 0:T], 1.0 / D),
                   waits=[t_sq, self.prev_tb[0]], inc=True)
        ps.release(b1, t2)
        tq = p.run("scalar", lambda e: e.activation(ex, mu, AF.Square),
                   waits=[t1, self.prev_tb[1]], inc=True)
        p.run("vector", lambda e: e.tensor_sub(va, va, ex), waits=[tq])
        t3 = p.run("vector", lambda e: e.tensor_scalar_add(va, va, LNEPS), inc=True)
        t4 = p.run("scalar", lambda e: e.activation(ex, va, AF.Sqrt), waits=[t3], inc=True)
        t5 = p.run("vector", lambda e: e.reciprocal(va, ex), waits=[t4], inc=True)
        p.run("vector", lambda e: e.tensor_mul(ex, mu, va))
        t6 = p.run("vector", lambda e: e.tensor_scalar_mul(ex, ex, -1.0), inc=True)
        rstd, nmr = va, ex
        tb1 = p.dma("gpsimd", self.bounce[0:1, t0:t0 + T], rstd, waits=[t5],
                    stream="lnw")
        tb2 = p.dma("gpsimd", self.bounce[1:2, t0:t0 + T], nmr, waits=[t6],
                    stream="lnw")
        tb3 = p.dma("gpsimd", self.LNB[:, 0:T], self.brow(0, t0, T),
                    waits=[tb2, ("ev", p.cnt["ev"]),
                           ("eg", p.cnt["eg"])], stream="lnb")
        tb4 = p.dma("gpsimd", self.LNB[:, T:2 * T], self.brow(1, t0, T),
                    stream="lnb")
        tb3 = tb4
        self.prev_tb = (tb2, tb2)
        out_toks = []
        for g in range(8):
            p.run("vector", lambda e, g=g: e.tensor_mul(tiles[g], tiles[g], self.LNB[:, 0:T]),
                  waits=[tb4, t_sq, t_sqall])
            out_toks.append(p.run("vector", lambda e, g=g: e.tensor_add(
                tiles[g], tiles[g], self.LNB[:, T:2 * T]), waits=[tb4], inc=True))
        return out_toks

    def emit_mamba(self, src, dst, src_ready=None):
        p, ps, nc = self.p, self.ps, self.nc
        G1, GH, G3, G4, G5 = self.G1, self.GH, self.G3, self.G4, self.G5
        ZST = self.ZST
        SLT, XD, BCB, HC = self.SLT, self.XD, self.BCB, self.HC
        LNB = self.LNB
        HALO = self.HALO

        def g1(g):
            return G1[:, g * T:(g + 1) * T]

        def gh(g):
            return GH[:, g * (T + 3):(g + 1) * (T + 3)]

        def slab(S, g):
            return S[:, g * T:(g + 1) * T]

        carry_tok = [self.t_zero] * 256
        xn_free = [self.t_zero] * 8
        yacc_free = [self.t_zero] * 16
        g3_free = self.t_zero
        halo_restore = [self.t_zero] * 16
        wb_free = [None, None]
        slot = {}
        zslot_free = [None, None]
        zread_free = [None, None]
        scan_es_guard = None

        for c in range(NCH):
            t0 = c * T
            srdy = src_ready[c] if src_ready else []
            xtok = [p.dma("sync", g1(g), src[128 * g:128 * (g + 1), t0:t0 + T],
                          waits=[xn_free[g], scan_es_guard] + srdy, stream="xl")
                    for g in range(8)]
            xtok = [xtok[-1]] * 8
            for g in range(16):
                if c == 0:
                    halo_restore[g] = p.run("vector", lambda e, g=g: e.memset(
                        gh(g)[:, 0:3], 0.0), inc=True)
                else:
                    halo_restore[g] = p.run("vector", lambda e, g=g: e.tensor_copy(
                        gh(g)[:, 0:3], HALO[:, 3 * g:3 * g + 3]), inc=True)

            xn_tok = self.emit_ln([g1(g) for g in range(8)], t0, xtok,
                                  lambda g: slab(G3, g), extra_read_tok=g3_free)

            xin_evac = [None] * 16
            for m in range(32):
                reg = m % 2
                wtok, waps = self.load_wblocks(reg, self.w_in_t, list(range(8)),
                                               128 * m, 128, wb_free)
                b = ps.acquire()
                pb = ps.banks[b]
                mms = [(lambda e, st, sp, k=k, ap=waps[k], pb=pb: e.matmul(
                    pb[:, 0:T], ap, g1(k), start=st, stop=sp)) for k in range(8)]
                tok = ps.group(b, mms, extra_waits=[wtok] + xn_tok)
                wb_free[reg] = tok
                if m < 16:
                    g = m
                    te = p.run("scalar", lambda e, g=g, pb=pb: e.activation(
                        gh(g)[:, 3:3 + T], pb[:, 0:T], AF.Identity, bias=self.BINCOL(g)),
                        waits=[tok, halo_restore[g], yacc_free[g]], inc=True)
                    xin_evac[g] = te
                else:
                    g = m - 16
                    zs = ZST[:, (g % 2) * T:(g % 2 + 1) * T]
                    te = p.run("scalar", lambda e, zs=zs, pb=pb, m=m: e.activation(
                        zs, pb[:, 0:T], AF.Silu, bias=self.BINCOL(m)),
                        waits=[tok, zslot_free[g % 2],
                               slot.get(("z", 2 + g % 2))], inc=True)
                    zslot_free[g % 2] = p.dma(
                        "gpsimd", self.zsilu[128 * g:128 * (g + 1), t0:t0 + T], zs,
                        waits=[te], stream=f"zw{g % 2}")
                ps.release(b, te)
            last_xz = p.now("tensor")
            for g in range(8):
                xn_free[g] = last_xz

            conv_s_tok = [None] * 16
            for g in range(16):
                p.run("vector", lambda e, g=g: e.tensor_scalar(
                    slab(G3, g), gh(g)[:, 0:T], self.TAP(g, 0), None, ALU.mult),
                    waits=[xin_evac[g]])
                for j in (1, 2, 3):
                    p.run("vector", lambda e, g=g, j=j: e.scalar_tensor_tensor(
                        slab(G3, g), gh(g)[:, j:j + T], self.TAP(g, j), slab(G3, g),
                        ALU.mult, ALU.add))
                tcv = p.run("vector", lambda e, g=g: e.tensor_copy(
                    HALO[:, 3 * g:3 * g + 3], gh(g)[:, T:T + 3]), inc=True)
                conv_s_tok[g] = p.run("scalar", lambda e, g=g: e.activation(
                    slab(G3, g), slab(G3, g), AF.Silu, bias=self.CBCOL(g)),
                    waits=[tcv], inc=True)
            t_halo_saved = p.now("vector")

            b = ps.acquire()
            pb = ps.banks[b]
            mmx = [(lambda e, st, sp, k=k, pb=pb: e.matmul(
                pb[0:96, 0:T], self.WXT[:, 96 * k:96 * (k + 1)], slab(G3, k),
                start=st, stop=sp)) for k in range(16)]
            tok = ps.group(b, mmx, extra_waits=conv_s_tok + [self.wxt_tok])
            t_xd = p.run("scalar", lambda e, pb=pb: e.activation(
                XD[0:96, 0:T], pb[0:96, 0:T], AF.Identity, bias=self.BXCOL),
                waits=[tok], inc=True)
            ps.release(b, t_xd)
            tbB = p.dma("gpsimd", self.bounce[2:18, t0:t0 + T], XD[64:80, 0:T],
                        waits=[t_xd], stream="bcw")
            tbC = p.dma("gpsimd", self.bounce[18:34, t0:t0 + T], XD[80:96, 0:T],
                        stream="bcw")
            tbB = tbC

            dt_tok = [None] * 16
            for m in range(16):
                b = ps.acquire()
                pb = ps.banks[b]
                tok = ps.group(b, [(lambda e, st, sp, m=m, pb=pb: e.matmul(
                    pb[:, 0:T], self.WDT[0:64, 128 * m:128 * (m + 1)], XD[0:64, 0:T],
                    start=st, stop=sp))], extra_waits=[t_xd, self.wdt_tok])
                p.run("scalar", lambda e, m=m, pb=pb: e.activation(
                    slab(G4, m), pb[:, 0:T], AF.Exp, bias=self.BDTCOL(m)),
                    waits=[tok])
                dt_tok[m] = p.run("scalar", lambda e, m=m: e.activation(
                    slab(G4, m), slab(G4, m), AF.Ln, bias=1.0), inc=True)
                ps.release(b, dt_tok[m])

            dtx_tok = [None] * 16
            for g in range(16):
                dtx_tok[g] = p.run("vector", lambda e, g=g: e.tensor_mul(
                    slab(G5, g), slab(G4, g), slab(G3, g)),
                    waits=[dt_tok[g], conv_s_tok[g]], inc=True)

            zstg = [("zw0", p.cnt.get("zw0", 0)), ("zw1", p.cnt.get("zw1", 0)),
                    ("zr0", p.cnt.get("zr0", 0)), ("zr1", p.cnt.get("zr1", 0))]
            yadd_tok = [[None] * 16 for _ in range(16)]
            pend_yadd = []
            u_last2 = [p.now("vector"), p.now("vector")]
            z_last2 = [p.now("vector"), p.now("vector")]
            for s in range(16):
                sp2 = s % 2
                bbuf = BCB[:, 0:T] if sp2 == 0 else LNB[:, 0:T]
                cbuf = BCB[:, T:2 * T] if sp2 == 0 else LNB[:, T:2 * T]
                tb_B = p.dma("gpsimd", bbuf,
                             self.brow(2 + s, t0, T), waits=[tbB, u_last2[sp2]],
                             stream="bcbB%d" % sp2)
                tb_C = p.dma("gpsimd", cbuf,
                             self.brow(18 + s, t0, T),
                             waits=[tbC, z_last2[sp2]], stream="bcbC%d" % sp2)
                for gb in range(8):
                    gl = [2 * gb + j for j in range(2)]
                    igl = [s * 16 + g for g in gl]
                    ipl = [i % 4 for i in igl]
                    dAl = [G1[:, ip * T:(ip + 1) * T] for ip in ipl]
                    ul = [G1[:, (4 + ip) * T:(5 + ip) * T] for ip in ipl]
                    hl = [SLT[:, (2 + ip) * T:(3 + ip) * T] for ip in ipl]
                    zl = [(SLT[:, (6 + ip) * T:(7 + ip) * T] if ip < 2
                           else ZST[:, (ip - 2) * T:(ip - 1) * T]) for ip in ipl]
                    t_dal, t_ul, t_hl = [], [], []
                    for j in range(len(gl)):
                        g, i_glob, ip = gl[j], igl[j], ipl[j]
                        t_dal.append(p.run("scalar",
                                     lambda e, g=g, s=s, dA=dAl[j]: e.activation(
                                         dA, slab(G4, g), AF.Exp,
                                         scale=self.ACOL(g, s)),
                                     waits=[self.t_acol, dt_tok[g],
                                            slot.get(("dA", ip)),
                                            last_xz if i_glob < 4 else None],
                                     inc=True))
                    for j in range(len(gl)):
                        g, i_glob, ip = gl[j], igl[j], ipl[j]
                        t_ul.append(p.run("vector",
                                    lambda e, g=g, u=ul[j], bbuf=bbuf: e.tensor_mul(
                                        u, slab(G5, g), bbuf),
                                    waits=[dtx_tok[g], tb_B, slot.get(("u", ip)),
                                           last_xz if i_glob < 4 else None],
                                    inc=True))
                        u_last2[sp2] = t_ul[-1]
                    for (pg, psn, pz1, ptz) in pend_yadd:
                        if psn == 0:
                            tk = p.dma("gpsimd", gh(pg)[:, 3:3 + T], pz1,
                                       waits=[ptz, yacc_free[pg]],
                                       stream=f"ya{pg}")
                        else:
                            tk = p.dma("gpsimd", gh(pg)[:, 3:3 + T], pz1,
                                       accum=ALU.add,
                                       waits=[ptz, yadd_tok[pg][psn - 1]],
                                       stream=f"ya{pg}")
                        yadd_tok[pg][psn] = tk
                        slot[("z", (16 * psn + pg) % 4)] = tk
                    pend_yadd = []
                    for j in range(len(gl)):
                        g, i_glob, ip = gl[j], igl[j], ipl[j]
                        idx = s * 16 + g
                        t_hl.append(p.run("vector",
                                    lambda e, dA=dAl[j], u=ul[j], h=hl[j],
                                    idx=idx: e.tensor_tensor_scan(
                                        h, dA, u, HC[:, idx:idx + 1],
                                        ALU.mult, ALU.add),
                                    waits=[t_dal[j], t_ul[j], carry_tok[idx],
                                           slot.get(("h", ip)),
                                           slot.get(("u", ip % 2))
                                           if i_glob < 4 else None], inc=True))
                        slot[("dA", ip)] = t_hl[j]
                        slot[("u", ip)] = t_hl[j]
                    idx0, ip0 = s * 16 + gl[0], ipl[0]
                    t_cy = p.run("scalar", lambda e, idx0=idx0, ip0=ip0: e.copy(
                        HC[:, idx0:idx0 + 2],
                        SLT[:, (3 + ip0) * T - 1:(4 + ip0) * T:T]),
                        waits=[t_hl[0], t_hl[1]], inc=True)
                    for j in range(len(gl)):
                        carry_tok[idx0 + j] = t_cy
                        slot[("h", ipl[j])] = t_cy
                    for j in range(len(gl)):
                        g, i_glob, ip = gl[j], igl[j], ipl[j]
                        t_z = p.run("vector", lambda e, h=hl[j], z1=zl[j],
                                    cbuf=cbuf: e.tensor_mul(z1, h, cbuf),
                                    waits=[tb_C, slot.get(("z", ip)),
                                           zstg[0] if ip >= 2 else None,
                                           zstg[1] if ip >= 2 else None,
                                           zstg[2] if ip >= 2 else None,
                                           zstg[3] if ip >= 2 else None],
                                    inc=True)
                        pend_yadd.append((g, s, zl[j], t_z))
                        z_last2[sp2] = t_z
            for (pg, psn, pz1, ptz) in pend_yadd:
                tk = p.dma("gpsimd", gh(pg)[:, 3:3 + T], pz1, accum=ALU.add,
                           waits=[ptz, yadd_tok[pg][psn - 1]], stream=f"ya{pg}")
                yadd_tok[pg][psn] = tk
                slot[("z", (16 * psn + pg) % 4)] = tk
            pend_yadd = []
            scan_done_ev = p.now("vector")
            scan_es_guard = p.now("scalar")
            for g in range(8):
                xn_free[g] = scan_done_ev
            yfin_tok = []
            for g in range(16):
                zs = ZST[:, (g % 2) * T:(g % 2 + 1) * T]
                tzl = p.dma("sync", zs, self.zsilu[128 * g:128 * (g + 1), t0:t0 + T],
                            waits=[zslot_free[g % 2], zread_free[g % 2],
                                   slot.get(("z", 2 + g % 2))],
                            stream=f"zr{g % 2}")
                p.run("vector", lambda e, g=g: e.scalar_tensor_tensor(
                    gh(g)[:, 3:3 + T], slab(G3, g), self.DCOL(g), gh(g)[:, 3:3 + T],
                    ALU.mult, ALU.add), waits=[yadd_tok[g][15]])
                tm = p.run("vector", lambda e, g=g, zs=zs: e.tensor_mul(
                    gh(g)[:, 3:3 + T], gh(g)[:, 3:3 + T], zs), waits=[tzl], inc=True)
                yfin_tok.append(tm)
                zread_free[g % 2] = tm
            self.t_g3w = p.now("vector")
            g3_free = self.t_g3w

            all_yadds = None
            yfin_now = p.now("vector")
            for m in range(8):
                b = ps.acquire()
                pb = ps.banks[b]
                p.wait("tensor", ps.free_tok[b])
                for t in yfin_tok:
                    p.wait("tensor", t)
                tok = None
                for half in range(2):
                    reg = (2 * m + half) % 2
                    wtok, waps = self.load_wblocks(reg, self.w_out_t,
                                                   list(range(8 * half, 8 * half + 8)),
                                                   128 * m, 128, wb_free)
                    p.wait("tensor", wtok)
                    for j in range(8):
                        k = 8 * half + j
                        tok = p.run("tensor", lambda e, k=k, ap=waps[j],
                                    st=(k == 0), sp=(half == 1 and j == 7),
                                    pb=pb: e.matmul(
                            pb[:, 0:T], ap, gh(k)[:, 3:3 + T], start=st, stop=sp),
                            inc=(j == 7))
                    wb_free[reg] = tok
                ta = SLT[:, (0 + m % 2) * T:(1 + m % 2) * T]
                xr = SLT[:, (2 + m % 2) * T:(3 + m % 2) * T]
                txr = p.dma("sync", xr, src[128 * m:128 * (m + 1), t0:t0 + T],
                            waits=[yfin_now, slot.get(("u", m % 2)),
                                   slot.get(("h", m % 2))] + srdy,
                            stream=f"xr{m % 2}")
                te = p.run("scalar", lambda e, pb=pb, m=m, ta=ta: e.activation(
                    ta, pb[:, 0:T], AF.Identity, bias=self.BOUTCOL(m)),
                    waits=[tok, slot.get(("z", m % 2)), slot.get(("dA", m % 2))],
                    inc=True)
                ps.release(b, te)
                tv = p.run("vector", lambda e, ta=ta, xr=xr: e.tensor_add(ta, ta, xr),
                           waits=[te, txr], inc=True)
                tst = p.dma("gpsimd", dst[128 * m:128 * (m + 1), t0:t0 + T], ta,
                            waits=[tv], stream=f"st{m % 2}")
                slot[("dA", m % 2)] = tst
                slot[("u", m % 2)] = tv
                if c == NCH - 1:
                    self.final_tokens.append(tst)
            self.store_toks.append([("st0", p.cnt.get("st0", 0)),
                                    ("st1", p.cnt.get("st1", 0))])
            t_lastw = p.now("tensor")
            for g in range(16):
                yacc_free[g] = t_lastw

    def load_wblocks(self, reg, wsrc, klist, m0, mwidth, wb_free):
        p = self.p
        aps, tok = [], None
        for j, k in enumerate(klist):
            dst = self.WB[:, reg * 1024 + j * mwidth: reg * 1024 + (j + 1) * mwidth]
            tok = p.dma("sync", dst, wsrc[128 * k:128 * (k + 1), m0:m0 + mwidth],
                        waits=[wb_free[reg]] if j == 0 else (),
                        stream=f"wb{reg}")
            aps.append(dst)
        return tok, aps


    def emit_ffn(self, src, dst, is_last, tag, src_ready=None):
        p, ps, nc = self.p, self.ps, self.nc
        G1, GH, G3, G4, G5 = self.G1, self.GH, self.G3, self.G4, self.G5
        SLT, XD, BCB, LNB, WB = self.SLT, self.XD, self.BCB, self.LNB, self.WB
        AX = mybir.AxisListType

        def g1(g):
            return G1[:, g * T:(g + 1) * T]

        def gh(g):
            return GH[:, g * (T + 3) + 3:g * (T + 3) + 3 + T]

        def slab(S, g):
            return S[:, g * T:(g + 1) * T]

        def dacc(mo):
            return G4[:, (4 + mo) * T:(5 + mo) * T]

        def comb(m):
            return G3[:, (8 + m) * T:(9 + m) * T]

        bnd = [p.now(k) for k in ("et", "ev", "es", "eg")]
        bcw_b = ("bcw", p.cnt.get("bcw", 0))
        xn_free = [bnd[0]] * 8
        g3_free = bnd[1]
        wslot = [None] * 4
        lnb_free = [bnd[1], bcw_b]
        rslot = {("ta", 0): ("st0", p.cnt.get("st0", 0)),
                 ("ta", 1): ("st1", p.cnt.get("st1", 0)),
                 ("xr", 0): ("xr0", p.cnt.get("xr0", 0)),
                 ("xr", 1): ("xr1", p.cnt.get("xr1", 0)),
                 ("rs", 0): bnd[1], ("rs", 1): bnd[1]}
        routed_rd = [bnd[0]] * 8
        comb_rd = bnd[0]
        lat_rd = [bnd[3]] * 4
        g5_rd = bnd[3]
        dacc_rd = [bnd[1]] * 8
        xd_rd = [bnd[0], bcw_b]
        wexp_rd = [bnd[3], bnd[1]]

        def wload(si, src_ap, waits=()):
            dstp = WB[:, si * 512:(si + 1) * 512]
            tok = p.dma("sync", dstp, src_ap,
                        waits=[wslot[si]] + list(waits), stream="ws%d%s" % (si, tag))
            return tok, dstp

        for c in range(NCH):
            t0 = c * T
            srdy = src_ready[c] if src_ready else []
            xtok = [p.dma("sync", g1(g), src[128 * g:128 * (g + 1), t0:t0 + T],
                          waits=[xn_free[g]] + srdy, stream="xl") for g in range(8)]
            xtok = [xtok[-1]] * 8
            xn_tok = self.emit_ln([g1(g) for g in range(8)], t0, xtok,
                                  lambda g: slab(G3, g), extra_read_tok=g3_free)

            b0 = ps.acquire()
            pb0 = ps.banks[b0]
            t_mm = ps.group(b0, [
                (lambda e, st, sp, k=k, pb0=pb0: e.matmul(
                    pb0[0:8, 0:T], self.GATE[:, 8 * k:8 * (k + 1)], g1(k),
                    start=st, stop=sp)) for k in range(8)],
                extra_waits=xn_tok + [self.ini_tok])
            t_sc = p.run("scalar", lambda e, pb0=pb0: e.activation(
                XD[0:8, 0:T], pb0[0:8, 0:T], AF.Sigmoid),
                waits=[t_mm] + xd_rd, inc=True)
            ps.release(b0, t_sc)

            bT = ps.acquire()
            pbT = ps.banks[bT]
            bW = ps.acquire()
            pbW = ps.banks[bW]
            p.wait("tensor", ps.free_tok[bT])
            p.wait("tensor", ps.free_tok[bW])
            w_toks = []
            prev_tv = None
            for j in range(4):
                tj = p.run("tensor", lambda e, j=j, pbT=pbT: e.transpose(
                    pbT[:, 8 * j:8 * j + 8], XD[0:8, 128 * j:128 * (j + 1)],
                    self.IDT[0:8, 0:8]), waits=[t_sc, prev_tv], inc=True)
                base = (j % 2) * T
                sct = SLT[:, base:base + 8]
                sfc = SLT[:, base + 8:base + 16]
                msk = SLT[:, base + 16:base + 24]
                wk = SLT[:, base + 24:base + 32]
                c1 = SLT[:, base + 32:base + 33]
                c2 = SLT[:, base + 33:base + 34]
                c3 = SLT[:, base + 34:base + 35]
                tv = p.run("vector", lambda e, sct=sct, pbT=pbT, j=j: e.tensor_copy(
                    sct, pbT[:, 8 * j:8 * j + 8]),
                    waits=[tj, rslot.get(("rt", j % 2)), rslot.get(("ta", j % 2)),
                           rslot.get(("xr", j % 2))])
                prev_tv = tv
                p.run("vector", lambda e, sfc=sfc, sct=sct: e.tensor_add(
                    sfc, sct, self.CORR), waits=[self.ini_tok])
                for gi in (0, 1):
                    gsl = sfc[:, 4 * gi:4 * gi + 4]
                    mg = msk[:, 4 * gi:4 * gi + 4]
                    gc = c1 if gi == 0 else c2
                    p.run("vector", lambda e, gsl=gsl, c3=c3: e.reduce_max(
                        c3, gsl, axis=AX.X))
                    p.run("vector", lambda e, gsl=gsl, mg=mg, c3=c3: e.tensor_scalar(
                        mg, gsl, c3, None, ALU.is_equal))
                    p.run("vector", lambda e, gsl=gsl, mg=mg: e.scalar_tensor_tensor(
                        mg, mg, -1e30, gsl, ALU.mult, ALU.add))
                    p.run("vector", lambda e, mg=mg, gc=gc: e.reduce_max(
                        gc, mg, axis=AX.X))
                    p.run("vector", lambda e, gc=gc, c3=c3: e.tensor_add(gc, gc, c3))
                p.run("vector", lambda e, c1=c1, c2=c2, c3=c3: e.tensor_tensor(
                    c3, c1, c2, ALU.is_ge))
                p.run("vector", lambda e, msk=msk, c3=c3: e.tensor_scalar(
                    msk[:, 0:4], self.ONES8[:, 0:4], c3, None, ALU.mult))
                p.run("vector", lambda e, msk=msk, c3=c3: e.tensor_scalar(
                    msk[:, 4:8], self.ONES8[:, 4:8], c3, None, ALU.subtract))
                p.run("vector", lambda e, sfc=sfc, msk=msk: e.tensor_mul(
                    sfc, sfc, msk))
                p.run("vector", lambda e, sfc=sfc, c1=c1: e.reduce_max(
                    c1, sfc, axis=AX.X))
                p.run("vector", lambda e, sfc=sfc, msk=msk, c1=c1: e.tensor_scalar(
                    msk, sfc, c1, None, ALU.is_equal))
                p.run("vector", lambda e, sfc=sfc, msk=msk: e.scalar_tensor_tensor(
                    msk, msk, -1e30, sfc, ALU.mult, ALU.add))
                p.run("vector", lambda e, msk=msk, c2=c2: e.reduce_max(
                    c2, msk, axis=AX.X))
                p.run("vector", lambda e, sfc=sfc, msk=msk, c2=c2: e.tensor_scalar(
                    msk, sfc, c2, None, ALU.is_ge))
                p.run("vector", lambda e, wk=wk, msk=msk, sct=sct: e.tensor_mul(
                    wk, msk, sct))
                p.run("vector", lambda e, wk=wk, c1=c1: e.reduce_sum(
                    c1, wk, axis=AX.X))
                p.run("vector", lambda e, c1=c1: e.tensor_scalar_add(c1, c1, 1e-20))
                p.run("vector", lambda e, c1=c1, c2=c2: e.reciprocal(c2, c1))
                tw = p.run("vector", lambda e, wk=wk, c2=c2: e.tensor_scalar(
                    wk, wk, c2, None, ALU.mult))
                tj2 = p.run("tensor", lambda e, j=j, pbW=pbW, wk=wk: e.transpose(
                    pbW[0:8, 128 * j:128 * (j + 1)], wk, self.IDT[:, :]),
                    waits=[tw], inc=True)
                w_toks.append(tj2)
                rslot[("rt", j % 2)] = tj2
            t_wev = p.run("scalar", lambda e, pbW=pbW: e.activation(
                BCB[0:8, 0:T], pbW[0:8, 0:T], AF.Copy),
                waits=w_toks + wexp_rd, inc=True)
            ps.release(bT, t_wev)
            ps.release(bW, t_wev)
            t_wb = p.dma("gpsimd", self.bounce[34:42, t0:t0 + T], BCB[0:8, 0:T],
                         waits=[t_wev], stream="wxw")
            wexp_rd = [t_wb]

            bl = [ps.acquire(pool=(0, 1, 2, 3)) for _ in range(4)]
            gtok = None
            for k in range(8):
                wtok, wap = wload(k % 2, self.fc1_t[128 * k:128 * (k + 1), 0:512])
                for m in range(4):
                    if k == 0:
                        p.wait("tensor", ps.free_tok[bl[m]])
                    gtok = p.run("tensor",
                                 lambda e, m=m, k=k, wap=wap, pbx=ps.banks[bl[m]]:
                                 e.matmul(pbx[:, 0:T],
                                          wap[:, 128 * m:128 * (m + 1)], g1(k),
                                          start=(k == 0), stop=(k == 7)),
                                 waits=[wtok] + (xn_tok if k == 0 else []),
                                 inc=(m == 3))
                wslot[k % 2] = ("et", p.cnt["et"])
            lat_tok = [None] * 4
            for m in range(4):
                lat_tok[m] = p.run("scalar", lambda e, m=m, pbx=ps.banks[bl[m]]:
                                   e.activation(slab(G4, m), pbx[:, 0:T], AF.Copy),
                                   waits=[gtok, lat_rd[m]], inc=True)
                ps.release(bl[m], lat_tok[m])

            rb = [0, 1, 2, 3]
            cb = [4, 5, 6, 7]
            for bb in cb:
                p.wait("tensor", ps.free_tok[bb])
            ctok = None
            rinfo = None

            def emit_rphase(ei):
                epar = ei % 2
                base = 4 * epar
                twb = p.dma("gpsimd", LNB[:, (ei % 2) * T:(ei % 2 + 1) * T],
                            self.brow(34 + ei, t0, T),
                            waits=[t_wb, lnb_free[ei % 2]], stream="web%d" % (ei % 2))
                rtok = None
                for k in range(4):
                    wtok, wap = wload(k % 2, self.w1f[512 * ei + 128 * k:
                                                      512 * ei + 128 * (k + 1), 0:512])
                    for m in range(4):
                        if k == 0:
                            p.wait("tensor", ps.free_tok[rb[m]])
                        rtok = p.run("tensor",
                                     lambda e, m=m, k=k, wap=wap,
                                     pbx=ps.banks[rb[m]]: e.matmul(
                                         pbx[:, 0:T],
                                         wap[:, 128 * m:128 * (m + 1)],
                                         slab(G4, k), start=(k == 0), stop=(k == 3)),
                                     waits=[wtok, lat_tok[k]],
                                     inc=(m == 3))
                    wslot[k % 2] = ("et", p.cnt["et"])
                g5w = [None] * 4
                for m in range(4):
                    rs = SLT[:, (4 + m % 2) * T:(5 + m % 2) * T]
                    t1 = p.run("scalar", lambda e, rs=rs, pbx=ps.banks[rb[m]]:
                               e.activation(rs, pbx[:, 0:T], AF.Relu),
                               waits=[rtok, rslot.get(("rs", m % 2))], inc=True)
                    ps.release(rb[m], t1)
                    p.run("scalar", lambda e, rs=rs: e.activation(
                        rs, rs, AF.Square))
                    g5w[m] = p.run("vector", lambda e, rs=rs, m=m, ei=ei,
                                   base=base: e.tensor_mul(
                                       slab(G5, base + m), rs,
                                       LNB[:, (ei % 2) * T:(ei % 2 + 1) * T]),
                                   waits=[("es", p.cnt["es"]), twb,
                                          g5_rd if ei < 2 else None,
                                          slot_g5.get(epar)], inc=True)
                    rslot[("rs", m % 2)] = g5w[m]
                lnb_free[ei % 2] = ("ev", p.cnt["ev"])
                return (ei, g5w, base)

            def emit_combphase(info):
                nonlocal ctok
                ei, g5w, base = info
                for k in range(4):
                    wtok, wap = wload(2 + k % 2,
                                      self.w2f[512 * ei + 128 * k:
                                               512 * ei + 128 * (k + 1), 0:512])
                    for m in range(4):
                        ctok = p.run("tensor",
                                     lambda e, m=m, k=k, wap=wap, base=base,
                                     st_=(ei == 0 and k == 0),
                                     sp_=(ei == 7 and k == 3),
                                     pbx=ps.banks[cb[m]]: e.matmul(
                                         pbx[:, 0:T],
                                         wap[:, 128 * m:128 * (m + 1)],
                                         slab(G5, base + k),
                                         start=st_, stop=sp_),
                                     waits=[wtok, g5w[k]],
                                     inc=(m == 3))
                    wslot[2 + k % 2] = ("et", p.cnt["et"])
                slot_g5[ei % 2] = ("et", p.cnt["et"])

            slot_g5 = {}
            for ei in range(8):
                nxt = emit_rphase(ei)
                if rinfo is not None:
                    emit_combphase(rinfo)
                rinfo = nxt
            emit_combphase(rinfo)
            comb_tok = [None] * 4
            for m in range(4):
                comb_tok[m] = p.run("scalar", lambda e, m=m, pbx=ps.banks[cb[m]]:
                                    e.activation(comb(m), pbx[:, 0:T], AF.Copy),
                                    waits=[ctok, comb_rd], inc=True)
                ps.release(cb[m], comb_tok[m])

            for mog in range(2):
                bf = [ps.acquire(pool=(0, 1, 2, 3)) for _ in range(4)]
                ftok = None
                for k in range(4):
                    wtok, wap = wload(k % 2, self.fc2_t[128 * k:128 * (k + 1),
                                                        512 * mog:512 * (mog + 1)])
                    for m in range(4):
                        if k == 0:
                            p.wait("tensor", ps.free_tok[bf[m]])
                        ftok = p.run("tensor",
                                     lambda e, m=m, k=k, wap=wap,
                                     pbx=ps.banks[bf[m]]: e.matmul(
                                         pbx[:, 0:T],
                                         wap[:, 128 * m:128 * (m + 1)],
                                         comb(k), start=(k == 0), stop=(k == 3)),
                                     waits=[wtok, comb_tok[k]],
                                     inc=(m == 3))
                    wslot[k % 2] = ("et", p.cnt["et"])
                for m in range(4):
                    mo = 4 * mog + m
                    tr = p.run("scalar", lambda e, mo=mo, pbx=ps.banks[bf[m]]:
                               e.activation(gh(mo), pbx[:, 0:T], AF.Copy),
                               waits=[ftok, routed_rd[mo]], inc=True)
                    ps.release(bf[m], tr)
                    routed_rd[mo] = tr
            comb_rd = ("et", p.cnt["et"])

            store_toks = []
            for half in range(2):
                for mug in range(4):
                    mub = 4 * half + mug
                    bu = [ps.acquire(pool=(0, 1, 2, 3)) for _ in range(4)]
                    utok = None
                    for k in range(8):
                        wtok, wap = wload(k % 2,
                                          self.up_t[128 * k:128 * (k + 1),
                                                    512 * mub:512 * (mub + 1)])
                        for m in range(4):
                            if k == 0:
                                p.wait("tensor", ps.free_tok[bu[m]])
                            utok = p.run("tensor",
                                         lambda e, m=m, k=k, wap=wap,
                                         pbx=ps.banks[bu[m]]: e.matmul(
                                             pbx[:, 0:T],
                                             wap[:, 128 * m:128 * (m + 1)], g1(k),
                                             start=(k == 0), stop=(k == 7)),
                                         waits=[wtok], inc=(m == 3))
                        wslot[k % 2] = ("et", p.cnt["et"])
                    for m in range(4):
                        iu = 4 * mug + m
                        t1 = p.run("scalar", lambda e, iu=iu, pbx=ps.banks[bu[m]]:
                                   e.activation(slab(G5, iu), pbx[:, 0:T], AF.Relu),
                                   waits=[utok, g5_rd if mug == 0 else None],
                                   inc=True)
                        t2 = p.run("scalar", lambda e, iu=iu: e.activation(
                            slab(G5, iu), slab(G5, iu), AF.Square))
                        ps.release(bu[m], ("es", p.cnt["es"]))
                up_done = ("es", p.cnt["es"])
                for mog in range(2):
                    bd = [ps.acquire(pool=(4, 5, 6, 7)) for _ in range(4)]
                    dtok = None
                    for k in range(16):
                        ku = 16 * half + k
                        wtok, wap = wload(2 + k % 2,
                                          self.down_t[128 * ku:128 * (ku + 1),
                                                      512 * mog:512 * (mog + 1)])
                        for m in range(4):
                            if k == 0:
                                p.wait("tensor", ps.free_tok[bd[m]])
                            dtok = p.run("tensor",
                                         lambda e, m=m, k=k, wap=wap,
                                         pbx=ps.banks[bd[m]]: e.matmul(
                                             pbx[:, 0:T],
                                             wap[:, 128 * m:128 * (m + 1)],
                                             slab(G5, k), start=(k == 0),
                                             stop=(k == 15)),
                                         waits=[wtok,
                                                up_done if k == 0 else None],
                                         inc=(m == 3))
                        wslot[2 + k % 2] = ("et", p.cnt["et"])
                    for m in range(4):
                        mo = 4 * mog + m
                        if half == 0:
                            td = p.run("scalar", lambda e, mo=mo,
                                       pbx=ps.banks[bd[m]]: e.activation(
                                           dacc(mo), pbx[:, 0:T], AF.Copy),
                                       waits=[dtok, dacc_rd[mo]], inc=True)
                            ps.release(bd[m], td)
                        else:
                            ta = SLT[:, (mo % 2) * T:(mo % 2 + 1) * T]
                            xr = SLT[:, (2 + mo % 2) * T:(3 + mo % 2) * T]
                            txr = p.dma("sync", xr,
                                        src[128 * mo:128 * (mo + 1), t0:t0 + T],
                                        waits=[rslot.get(("xr", mo % 2)),
                                               ("ev", p.cnt["ev"])] + srdy,
                                        stream="xr%d" % (mo % 2))
                            t1 = p.run("vector", lambda e, ta=ta, mo=mo,
                                       pbx=ps.banks[bd[m]]: e.tensor_add(
                                           ta, pbx[:, 0:T], dacc(mo)),
                                       waits=[dtok, rslot.get(("ta", mo % 2))],
                                       inc=True)
                            ps.release(bd[m], t1)
                            dacc_rd[mo] = t1
                            p.run("vector", lambda e, ta=ta, mo=mo: e.tensor_add(
                                ta, ta, gh(mo)), waits=[routed_rd[mo]])
                            routed_rd[mo] = ("ev", p.cnt["ev"])
                            tv = p.run("vector", lambda e, ta=ta, xr=xr:
                                       e.tensor_add(ta, ta, xr),
                                       waits=[txr], inc=True)
                            rslot[("xr", mo % 2)] = tv
                            tst = p.dma("gpsimd",
                                        dst[128 * mo:128 * (mo + 1), t0:t0 + T],
                                        ta, waits=[tv], stream="st%d" % (mo % 2))
                            rslot[("ta", mo % 2)] = tst
                            if c == NCH - 1 and is_last:
                                self.final_tokens.append(tst)
                g5_rd = ("et", p.cnt["et"])
            self.store_toks.append([("st0", p.cnt.get("st0", 0)),
                                    ("st1", p.cnt.get("st1", 0))])
            last_read_xn = ("et", p.cnt["et"])
            for g in range(8):
                xn_free[g] = last_read_xn
            g3_free = ("ev", p.cnt["ev"])
            xd_rd = [("et", p.cnt["et"])]


    def emit_attn(self, src, dst, is_last, src_ready=None):
        p, ps, nc = self.p, self.ps, self.nc
        G1, GH, G3, G4, G5 = self.G1, self.GH, self.G3, self.G4, self.G5
        SLT, XD, BCB, LNB, WB, HALO = (self.SLT, self.XD, self.BCB, self.LNB,
                                       self.WB, self.HALO)

        def g1(g):
            return G1[:, g * T:(g + 1) * T]

        def ghx(g):
            return GH[:, g * (T + 3):(g + 1) * (T + 3)]

        def slab(S, g):
            return S[:, g * T:(g + 1) * T]

        bnd = [p.now(k) for k in ("et", "ev", "es", "eg")]
        BWMX = self.ZST
        t_bw = p.dma("sync", BWMX[:, 0:64],
                     bass.AP(self.bw_.tensor, 0, [[8, 128], [1024, 8], [1, 8]]),
                     waits=[("zr0", p.cnt.get("zr0", 0)),
                            ("zr1", p.cnt.get("zr1", 0))] + bnd,
                     stream="bwx")
        t_bw = p.dma("sync", BWMX[:, 64:128],
                     bass.AP(self.mixw_.tensor, 0, [[8, 128], [1024, 8], [1, 8]]),
                     stream="bwx")
        xn_free = [bnd[0]] * 8
        g3_free = bnd[1]
        wslot = [None] * 4
        rslot = {("ta", 0): ("st0", p.cnt.get("st0", 0)),
                 ("ta", 1): ("st1", p.cnt.get("st1", 0)),
                 ("xr", 0): ("xr0", p.cnt.get("xr0", 0)),
                 ("xr", 1): ("xr1", p.cnt.get("xr1", 0)),
                 ("sq", 0): bnd[1], ("sq", 1): bnd[1],
                 ("kb", 0): ("st0", p.cnt.get("st0", 0)),
                 ("kb", 1): ("st1", p.cnt.get("st1", 0))}
        gh_free = [bnd[0]] * 8
        g3fr = [[] for _ in range(8)]
        halo_tok = [bnd[1]] * 24
        lnb_free = [bnd[1], ("web0", p.cnt.get("web0", 0))]

        def wload(si, src_ap, waits=()):
            dstp = WB[:, si * 512:(si + 1) * 512]
            tok = p.dma("sync", dstp, src_ap,
                        waits=[wslot[si]] + list(waits), stream="ws%db" % si)
            return tok, dstp

        def wcols4(wt, k0, m):
            return bass.AP(wt.tensor, 128 * k0 * 1024 + 128 * m,
                           [[1024, 128], [128 * 1024, 4], [1, 128]])

        beta_rows = None
        for c in range(NCH):
            t0 = c * T
            srdy = src_ready[c] if src_ready else []
            xtok = [p.dma("sync", g1(g), src[128 * g:128 * (g + 1), t0:t0 + T],
                          waits=[xn_free[g]] + srdy, stream="xl") for g in range(8)]
            xtok = [xtok[-1]] * 8
            xn_tok = self.emit_ln([g1(g) for g in range(8)], t0, xtok,
                                  lambda g: slab(G3, g), extra_read_tok=g3_free)

            for woff, bias, rowb, tagw in ((0, None, 58, "betw"),
                                           (64, self.MIXB, 66, "gw")):
                bb = ps.acquire()
                pbb = ps.banks[bb]
                tm = ps.group(bb, [
                    (lambda e, st, sp, k=k, pbb=pbb, woff=woff: e.matmul(
                        pbb[0:8, 0:T],
                        BWMX[:, woff + 8 * k:woff + 8 * (k + 1)], g1(k),
                        start=st, stop=sp)) for k in range(8)],
                    extra_waits=xn_tok + [self.ini_tok, self.mixb_tok, t_bw])
                te = p.run("scalar", lambda e, pbb=pbb, bias=bias: e.activation(
                    XD[0:8, 0:T], pbb[0:8, 0:T], AF.Sigmoid,
                    **({} if bias is None else {"bias": bias})),
                    waits=[tm, rslot.get("xd")], inc=True)
                ps.release(bb, te)
                tb = p.dma("gpsimd", bass.AP(self.bounce.tensor, rowb * L + t0,
                                             [[L, 8], [1, T]]), XD[0:8, 0:T],
                           waits=[te], stream=tagw)
                rslot["xd"] = tb
                if rowb == 58:
                    beta_rows = tb
                else:
                    self.g_rows_tok = tb

            for ti, (wt, scr) in enumerate(((self.qw_, self.q_s),
                                            (self.kw_, self.k_s),
                                            (self.vw_, self.v_s))):
                ghtok = [None] * 8
                for m in range(8):
                    bq = ps.acquire()
                    pbq = ps.banks[bq]
                    gtok = None
                    wap = None
                    for k in range(8):
                        if k % 4 == 0:
                            si = (2 * ti + k // 4) % 4
                            wtok, wap = wload(si, wcols4(wt, k, m))
                        if k == 0:
                            p.wait("tensor", ps.free_tok[bq])
                        gtok = p.run("tensor",
                                     lambda e, k=k, wap=wap, pbq=pbq: e.matmul(
                                         pbq[:, 0:T],
                                         wap[:, 128 * (k % 4):128 * (k % 4 + 1)],
                                         g1(k), start=(k == 0), stop=(k == 7)),
                                     waits=[wtok] + (xn_tok if k == 0 else []),
                                     inc=(k % 4 == 3))
                        if k % 4 == 3:
                            wslot[(2 * ti + k // 4) % 4] = ("et", p.cnt["et"])
                    hidx = ti * 8 + m
                    if c == 0:
                        th = p.run("vector", lambda e, m=m: e.memset(
                            ghx(m)[:, 0:3], 0.0), waits=[gh_free[m]])
                    else:
                        th = p.run("vector", lambda e, m=m, hidx=hidx: e.tensor_copy(
                            ghx(m)[:, 0:3], HALO[:, 3 * hidx:3 * hidx + 3]),
                            waits=[gh_free[m], halo_tok[hidx]])
                    ghtok[m] = p.run("scalar", lambda e, m=m, pbq=pbq: e.activation(
                        ghx(m)[:, 3:3 + T], pbq[:, 0:T], AF.Copy),
                        waits=[gtok, th], inc=True)
                    ps.release(bq, ghtok[m])
                conv_tok = [None] * 8
                for m in range(8):
                    hidx = ti * 8 + m
                    p.run("vector", lambda e, m=m, ti=ti: e.tensor_scalar(
                        slab(G3, 8 + m), ghx(m)[:, 0:T], self.CTAP(ti, m, 0), None,
                        ALU.mult),
                        waits=[ghtok[m], self.ini_tok, g3_free, bnd[0]] + g3fr[m])
                    tcv = None
                    for j in (1, 2, 3):
                        tcv = p.run("vector", lambda e, m=m, j=j, ti=ti:
                                    e.scalar_tensor_tensor(
                                        slab(G3, 8 + m), ghx(m)[:, j:j + T],
                                        self.CTAP(ti, m, j), slab(G3, 8 + m),
                                        ALU.mult, ALU.add))
                    halo_tok[hidx] = p.run("vector",
                                           lambda e, m=m, hidx=hidx: e.tensor_copy(
                                               HALO[:, 3 * hidx:3 * hidx + 3],
                                               ghx(m)[:, T:T + 3]), inc=True)
                    gh_free[m] = halo_tok[hidx]
                    conv_tok[m] = p.run("scalar", lambda e, m=m: e.activation(
                        slab(G3, 8 + m), slab(G3, 8 + m), AF.Silu),
                        waits=[tcv], inc=True)
                if ti < 2:
                    rowb = 42 if ti == 0 else 50
                    for m in range(8):
                        sqa = SLT[:, (4 + m % 2) * T:(5 + m % 2) * T]
                        tsq = p.run("scalar", lambda e, m=m, sqa=sqa: e.activation(
                            sqa, slab(G3, 8 + m), AF.Square),
                            waits=[conv_tok[m], rslot.get(("sq", m % 2))], inc=True)
                        bs = ps.acquire()
                        pbs = ps.banks[bs]
                        tms = ps.group(bs, [(lambda e, st, sp, sqa=sqa, pbs=pbs:
                                             e.matmul(pbs[0:1, 0:T], self.ONES, sqa,
                                                      start=st, stop=sp))],
                                       extra_waits=[tsq])
                        r1 = SLT[0:1, (4 + m % 2) * T:(5 + m % 2) * T]
                        tv1 = p.run("vector", lambda e, r1=r1, pbs=pbs:
                                    e.tensor_scalar_add(r1, pbs[0:1, 0:T], 1e-12),
                                    waits=[tms], inc=True)
                        ps.release(bs, tv1)
                        tv2 = p.run("scalar", lambda e, r1=r1: e.activation(
                            r1, r1, AF.Sqrt), waits=[tv1], inc=True)
                        tv3 = p.run("vector", lambda e, r1=r1: e.reciprocal(r1, r1),
                                    waits=[tv2], inc=True)
                        tbr = p.dma("gpsimd",
                                    bass.AP(self.bounce.tensor,
                                            (rowb + m) * L + t0, [[L, 1], [1, T]]),
                                    r1, waits=[tv3], stream="rnw%d" % ti)
                        rslot[("sq", m % 2)] = tbr
                        tbc = p.dma("gpsimd", LNB[:, (m % 2) * T:(m % 2 + 1) * T],
                                    self.brow(rowb + m, t0, T),
                                    waits=[tbr, lnb_free[m % 2]],
                                    stream="rnb%d" % (m % 2))
                        tsc = p.run("vector", lambda e, m=m: e.tensor_mul(
                            slab(G3, 8 + m), slab(G3, 8 + m),
                            LNB[:, (m % 2) * T:(m % 2 + 1) * T]),
                            waits=[tbc], inc=True)
                        lnb_free[m % 2] = tsc
                        conv_tok[m] = tsc
                tqs = None
                for m in range(8):
                    tqs = p.dma("gpsimd", scr[128 * m:128 * (m + 1), t0:t0 + T],
                                slab(G3, 8 + m), waits=[conv_tok[m]],
                                stream="qs%d" % ti)
                for m in range(8):
                    g3fr[m] = [tqs]
                if ti >= 1:
                    dscr = self.kb_s if ti == 1 else self.vb_s
                    for m in range(8):
                        tbc = p.dma("gpsimd", LNB[:, (m % 2) * T:(m % 2 + 1) * T],
                                    self.brow(58 + m, t0, T),
                                    waits=[beta_rows, lnb_free[m % 2]],
                                    stream="rnb%d" % (m % 2))
                        kslt = SLT[:, (m % 2) * T:(m % 2 + 1) * T]
                        tkb = p.run("vector", lambda e, m=m, kslt=kslt: e.tensor_mul(
                            kslt, slab(G3, 8 + m),
                            LNB[:, (m % 2) * T:(m % 2 + 1) * T]),
                            waits=[tbc, conv_tok[m], rslot.get(("kb", m % 2))],
                            inc=True)
                        g3fr[m].append(tkb)
                        lnb_free[m % 2] = tkb
                        tskb = p.dma("gpsimd",
                                     dscr[128 * m:128 * (m + 1), t0:t0 + T],
                                     kslt, waits=[tkb], stream="kbs%d" % (m % 2))
                        rslot[("kb", m % 2)] = tskb
            last_xn_rd = ("et", p.cnt["et"])
            for g in range(8):
                xn_free[g] = last_xn_rd
            g3_free = ("ev", p.cnt["ev"])

        qkv_all = [("qs0", p.cnt.get("qs0", 0)), ("qs1", p.cnt.get("qs1", 0)),
                   ("kbs0", p.cnt.get("kbs0", 0)), ("kbs1", p.cnt.get("kbs1", 0)),
                   ("qs2", p.cnt.get("qs2", 0))]

        if "B" not in ATTN_PHASES:
            return
        CC = 128
        NCC = (NCH * T) // CC
        t_s0 = p.run("vector", lambda e: e.memset(G4[:, 0:1024], 0.0),
                     waits=bnd + [("eg", p.cnt["eg"])], inc=True)
        S_tok = [t_s0] * 8

        def g5t(i, par):
            base = par * 16
            return G5[:, (base + i) * 128:(base + i + 1) * 128]

        slotB = {}
        for par in range(4):
            for ii in range(4):
                slotB[(ii, par)] = bnd[0]
            slotB[("z", par)] = bnd[1]
            slotB[("X", par)] = bnd[1]
            slotB[("Y", par)] = bnd[1]
            slotB[("nw", par)] = bnd[1]
            slotB[("u1", par)] = bnd[1]
            slotB[("at", par)] = bnd[1]
            slotB[("ob", par)] = bnd[1]
        def b_iter(ci, h):
            par = (ci * 8 + h) % 4
            bk_tgl = [0]

            def nb():
                i2 = 2 * par + bk_tgl[0]
                bk_tgl[0] ^= 1
                return i2
            qt, kt, kbt, vbt = (g5t(i, par) for i in range(4))
            zb = G5[:, (par * 16 + 4) * 128:(par * 16 + 7) * 128]
            Xc, Yc, Xn, Yn = (g5t(i, par) for i in range(7, 11))
            negw, U1, attnT, OB = (g5t(i, par) for i in range(11, 15))
            Sh = G4[:, 128 * h:128 * (h + 1)]
            ld = None
            for ii, scr in enumerate((self.q_s, self.k_s, self.kb_s,
                                      self.vb_s)):
                ld = p.dma("sync", g5t(ii, par),
                           scr[128 * h:128 * (h + 1), CC * ci:CC * (ci + 1)],
                           waits=[slotB[(ii, par)], qkv_all[ii] if ii < 4 else None],
                           stream="bl%d" % par)
            ldt = ld
            yield
            bM = nb()
            pbM = ps.banks[bM]
            p.wait("tensor", ps.free_tok[bM])
            p.run("tensor", lambda e, pbM=pbM, kbt=kbt, kt=kt: e.matmul(
                pbM[:, 0:128], kbt, kt, start=True, stop=True), waits=[ldt])
            tM = p.run("tensor", lambda e, pbM=pbM, kbt=kbt, kt=kt: e.matmul(
                pbM[:, 128:256], kt, kbt, start=True, stop=True), inc=True)
            yield
            tN = p.run("vector", lambda e, pbM=pbM, Xc=Xc: e.tensor_mul(
                Xc, pbM[:, 0:128], self.LMSK[:, :]),
                waits=[tM, slotB[("X", par)], self.ini_tok], inc=True)
            tY = p.run("vector", lambda e, pbM=pbM, Yc=Yc: e.tensor_mul(
                Yc, pbM[:, 128:256], self.UMSK[:, :]),
                waits=[slotB[("Y", par)]], inc=True)
            ps.release(bM, tY)
            bZ = nb()
            pbZ = ps.banks[bZ]
            p.wait("tensor", ps.free_tok[bZ])
            p.run("tensor", lambda e, pbZ=pbZ, vbt=vbt: e.transpose(
                pbZ[:, 0:128], vbt, self.IDT[:, :]), waits=[ldt])
            p.run("tensor", lambda e, pbZ=pbZ, kbt=kbt: e.transpose(
                pbZ[:, 128:256], kbt, self.IDT[:, :]))
            tz3 = p.run("tensor", lambda e, pbZ=pbZ, kt=kt: e.transpose(
                pbZ[:, 256:384], kt, self.IDT[:, :]), inc=True)
            yield
            zt = p.run("vector", lambda e, zb=zb, pbZ=pbZ: e.tensor_copy(
                zb, pbZ[:, 0:384]),
                waits=[tz3, slotB[("z", par)]], inc=True)
            ps.release(bZ, zt)
            xk, yk = Xc, Yc
            txk, tyk = tN, tY
            for kst in range(7):
                bA = nb()
                pbA = ps.banks[bA]
                p.wait("tensor", ps.free_tok[bA])
                ta = p.run("tensor", lambda e, pbA=pbA, yk=yk, zb=zb: e.matmul(
                    pbA[:, 0:256], yk, zb[:, 0:256], start=True, stop=True),
                    waits=[tyk, zt], inc=True)
                if kst < 6:
                    bS = nb()
                    pbS = ps.banks[bS]
                    p.wait("tensor", ps.free_tok[bS])
                    p.run("tensor", lambda e, pbS=pbS, yk=yk, xk=xk: e.matmul(
                        pbS[:, 0:128], yk, xk, start=True, stop=True),
                        waits=[txk, tyk])
                    ts = p.run("tensor", lambda e, pbS=pbS, yk=yk, xk=xk:
                               e.matmul(pbS[:, 128:256], xk, yk,
                                        start=True, stop=True), inc=True)
                yield
                zt = p.run("vector", lambda e, zb=zb, pbA=pbA: e.tensor_add(
                    zb[:, 0:256], pbA[:, 0:256], zb[:, 0:256]),
                    waits=[ta], inc=True)
                ps.release(bA, zt)
                if kst < 6:
                    nxk = Xn if (kst % 2 == 0) else Xc
                    nyk = Yn if (kst % 2 == 0) else Yc
                    txk = p.run("vector", lambda e, nxk=nxk, pbS=pbS:
                                e.tensor_copy(nxk, pbS[:, 0:128]),
                                waits=[ts], inc=True)
                    tyk = p.run("scalar", lambda e, nyk=nyk, pbS=pbS:
                                e.activation(nyk, pbS[:, 128:256], AF.Copy),
                                waits=[ts, txk], inc=True)
                    ps.release(bS, tyk)
                    xk, yk = nxk, nyk
            slotB[("X", par)] = ("et", p.cnt["et"])
            slotB[("Y", par)] = ("et", p.cnt["et"])
            bW2 = nb()
            pbW2 = ps.banks[bW2]
            p.wait("tensor", ps.free_tok[bW2])
            tw1 = p.run("tensor", lambda e, pbW2=pbW2, zb=zb: e.transpose(
                pbW2[:, 0:128], zb[:, 128:256], self.IDT[:, :]),
                waits=[zt], inc=True)
            yield
            tw2 = p.run("vector", lambda e, negw=negw, pbW2=pbW2:
                        e.tensor_scalar_mul(negw, pbW2[:, 0:128], -1.0),
                        waits=[tw1, slotB[("nw", par)]], inc=True)
            ps.release(bW2, tw2)
            bU = nb()
            pbU = ps.banks[bU]
            p.wait("tensor", ps.free_tok[bU])
            p.run("tensor", lambda e, pbU=pbU, negw=negw, Sh=Sh: e.matmul(
                pbU[:, 0:128], negw, Sh, start=True, stop=False),
                waits=[tw2, S_tok[h]])
            tu = p.run("tensor", lambda e, pbU=pbU, zb=zb: e.matmul(
                pbU[:, 0:128], self.IDT[:, :], zb[:, 0:128],
                start=False, stop=True), inc=True)
            yield
            tu1 = p.run("scalar", lambda e, U1=U1, pbU=pbU: e.activation(
                U1, pbU[:, 0:128], AF.Copy),
                waits=[tu, slotB[("u1", par)]], inc=True)
            ps.release(bU, tu1)
            bAT = nb()
            pbAT = ps.banks[bAT]
            p.wait("tensor", ps.free_tok[bAT])
            tat = p.run("tensor", lambda e, pbAT=pbAT, kt=kt, qt=qt: e.matmul(
                pbAT[:, 0:128], kt, qt, start=True, stop=True),
                waits=[ldt], inc=True)
            yield
            tat2 = p.run("vector", lambda e, attnT=attnT, pbAT=pbAT:
                         e.tensor_mul(attnT, pbAT[:, 0:128], self.UIMSK[:, :]),
                         waits=[tat, slotB[("at", par)]], inc=True)
            ps.release(bAT, tat2)
            bO = nb()
            pbO = ps.banks[bO]
            p.wait("tensor", ps.free_tok[bO])
            p.run("tensor", lambda e, pbO=pbO, qt=qt, Sh=Sh: e.matmul(
                pbO[:, 0:128], Sh, qt, start=True, stop=False),
                waits=[S_tok[h], tu1])
            to = p.run("tensor", lambda e, pbO=pbO, U1=U1, attnT=attnT:
                       e.matmul(pbO[:, 0:128], U1, attnT,
                                start=False, stop=True),
                       waits=[tat2], inc=True)
            yield
            tob = p.run("scalar", lambda e, OB=OB, pbO=pbO: e.activation(
                OB, pbO[:, 0:128], AF.Copy),
                waits=[to, slotB[("ob", par)]], inc=True)
            ps.release(bO, tob)
            tos = p.dma("gpsimd", self.o_s[128 * h:128 * (h + 1),
                                           CC * ci:CC * (ci + 1)], OB,
                        waits=[tob], stream="os%d" % par)
            slotB[("ob", par)] = tos
            bSU = nb()
            pbSU = ps.banks[bSU]
            p.wait("tensor", ps.free_tok[bSU])
            tsu = p.run("tensor", lambda e, pbSU=pbSU, zb=zb, U1=U1: e.matmul(
                pbSU[:, 0:128], zb[:, 256:384], U1, start=True, stop=True),
                waits=[tu1], inc=True)
            yield
            tsa = p.run("vector", lambda e, pbSU=pbSU, Sh=Sh: e.tensor_add(
                Sh, pbSU[:, 0:128], Sh), waits=[tsu, to], inc=True)
            ps.release(bSU, tsa)
            S_tok[h] = tsa
            etn = ("et", p.cnt["et"])
            for ii in range(4):
                slotB[(ii, par)] = etn
            slotB[("z", par)] = tsu
            slotB[("nw", par)] = etn
            slotB[("u1", par)] = tsu
            slotB[("at", par)] = to

        n_it = min(NCC * 8, B_LIMIT)
        for blk in range(0, n_it, 4):
            gens = [b_iter((blk + j) // 8, (blk + j) % 8)
                    for j in range(min(4, n_it - blk))]
            live = list(gens)
            while live:
                nxt = []
                for gme in live:
                    try:
                        next(gme)
                        nxt.append(gme)
                    except StopIteration:
                        pass
                live = nxt
        o_all = [("os%d" % j, p.cnt.get("os%d" % j, 0)) for j in range(4)]

        if "C" not in ATTN_PHASES:
            return
        vt_all = ("qs2", p.cnt.get("qs2", 0))
        rms_free = [("rnb0", p.cnt.get("rnb0", 0)), ("rnb1", p.cnt.get("rnb1", 0))]
        bcb_guard = [bnd[1], ("wxw", p.cnt.get("wxw", 0)),
                     ("bcb", p.cnt.get("bcb", 0))]
        g3v_free = ("ev", p.cnt["ev"])
        for c in range(NCH):
            t0 = c * T
            otok = [p.dma("sync", g1(g), self.o_s[128 * g:128 * (g + 1), t0:t0 + T],
                          waits=o_all + [xn_free[g]], stream="ol")
                    for g in range(8)]
            otok = [otok[-1]] * 8
            vtok = [p.dma("sync", slab(G3, 8 + g),
                          self.v_s[128 * g:128 * (g + 1), t0:t0 + T],
                          waits=[vt_all, g3v_free], stream="vl") for g in range(8)]
            vtok = [vtok[-1]] * 8
            go_tok = []
            for m in range(8):
                tbc = p.dma("gpsimd", LNB[:, (m % 2) * T:(m % 2 + 1) * T],
                            self.brow(66 + m, t0, T),
                            waits=[self.g_rows_tok, lnb_free[m % 2]],
                            stream="rnb%d" % (m % 2))
                p.run("vector", lambda e, m=m: e.tensor_sub(
                    g1(m), g1(m), slab(G3, 8 + m)), waits=[otok[m], vtok[m]])
                p.run("vector", lambda e, m=m: e.tensor_mul(
                    g1(m), g1(m), LNB[:, (m % 2) * T:(m % 2 + 1) * T]),
                    waits=[tbc])
                t2 = p.run("vector", lambda e, m=m: e.tensor_add(
                    g1(m), g1(m), slab(G3, 8 + m)), inc=True)
                lnb_free[m % 2] = t2
                sqa = SLT[:, (4 + m % 2) * T:(5 + m % 2) * T]
                tsq = p.run("scalar", lambda e, m=m, sqa=sqa: e.activation(
                    sqa, g1(m), AF.Square),
                    waits=[t2, rslot.get(("sq", m % 2))], inc=True)
                bs = ps.acquire()
                pbs = ps.banks[bs]
                tms = ps.group(bs, [(lambda e, st, sp, sqa=sqa, pbs=pbs: e.matmul(
                    pbs[0:1, 0:T], self.ONES, sqa, start=st, stop=sp))],
                    extra_waits=[tsq])
                r1 = SLT[0:1, (4 + m % 2) * T:(5 + m % 2) * T]
                tv1 = p.run("vector", lambda e, r1=r1, pbs=pbs: e.tensor_scalar(
                    r1, pbs[0:1, 0:T], 1.0 / 128.0, LNEPS, ALU.mult, ALU.add),
                    waits=[tms], inc=True)
                ps.release(bs, tv1)
                tv2 = p.run("scalar", lambda e, r1=r1: e.activation(
                    r1, r1, AF.Sqrt), waits=[tv1], inc=True)
                tv3 = p.run("vector", lambda e, r1=r1: e.reciprocal(r1, r1),
                            waits=[tv2], inc=True)
                tbr = p.dma("gpsimd",
                            bass.AP(self.bounce.tensor, (42 + m) * L + t0,
                                    [[L, 1], [1, T]]), r1,
                            waits=[tv3, ("rnb0", p.cnt.get("rnb0", 0)) if c == 0
                                   else None], stream="rmw")
                rslot[("sq", m % 2)] = tbr
                tbc2 = p.dma("gpsimd", BCB[:, (m % 2) * T:(m % 2 + 1) * T],
                             self.brow(42 + m, t0, T),
                             waits=[tbr, rms_free[m % 2]] + bcb_guard,
                             stream="rmb%d" % (m % 2))
                gt = p.run("vector", lambda e, m=m: e.scalar_tensor_tensor(
                    g1(m), g1(m), self.WNORM, BCB[:, (m % 2) * T:(m % 2 + 1) * T],
                    ALU.mult, ALU.mult), waits=[tbc2, self.ini_tok], inc=True)
                rms_free[m % 2] = gt
                go_tok.append(gt)
            srdy = src_ready[c] if src_ready else []
            for mo in range(8):
                bq = ps.acquire()
                pbq = ps.banks[bq]
                gtok = None
                wap = None
                for k in range(8):
                    if k % 4 == 0:
                        si = (2 * mo + k // 4) % 4
                        wtok, wap = wload(si, wcols4(self.ow_, k, mo))
                    if k == 0:
                        p.wait("tensor", ps.free_tok[bq])
                    gtok = p.run("tensor", lambda e, k=k, wap=wap, pbq=pbq: e.matmul(
                        pbq[:, 0:T], wap[:, 128 * (k % 4):128 * (k % 4 + 1)],
                        g1(k), start=(k == 0), stop=(k == 7)),
                        waits=[wtok] + (go_tok if k == 0 else []),
                        inc=(k % 4 == 3))
                    if k % 4 == 3:
                        wslot[(2 * mo + k // 4) % 4] = ("et", p.cnt["et"])
                ta = SLT[:, (mo % 2) * T:(mo % 2 + 1) * T]
                xr = SLT[:, (2 + mo % 2) * T:(3 + mo % 2) * T]
                txr = p.dma("sync", xr, src[128 * mo:128 * (mo + 1), t0:t0 + T],
                            waits=[rslot.get(("xr", mo % 2))] + srdy,
                            stream="xr%d" % (mo % 2))
                te = p.run("scalar", lambda e, pbq=pbq, ta=ta: e.activation(
                    ta, pbq[:, 0:T], AF.Copy),
                    waits=[gtok, rslot.get(("ta", mo % 2)),
                           rslot.get(("kb", mo % 2))], inc=True)
                ps.release(bq, te)
                tv = p.run("vector", lambda e, ta=ta, xr=xr: e.tensor_add(
                    ta, ta, xr), waits=[te, txr], inc=True)
                rslot[("xr", mo % 2)] = tv
                tst = p.dma("gpsimd", dst[128 * mo:128 * (mo + 1), t0:t0 + T],
                            ta, waits=[tv], stream="st%d" % (mo % 2))
                rslot[("ta", mo % 2)] = tst
                if c == NCH - 1 and is_last:
                    self.final_tokens.append(tst)
            etc = ("et", p.cnt["et"])
            for g in range(8):
                xn_free[g] = etc
            g3v_free = ("ev", p.cnt["ev"])
            self.store_toks.append([("st0", p.cnt.get("st0", 0)),
                                    ("st1", p.cnt.get("st1", 0))])

    def _build(self):
        nc, p = self.nc, self.p
        din, sbuf = self.din, self.sbuf

        self.x_in = din("x_in", [D, L])
        self.w_in_t = din("w_in_t", [D, 2 * DIN])
        b_in_c = din("b_in_c", [128, 32])
        conv_w_r = din("conv_w_r", [128, 64])
        conv_b_r = din("conv_b_r", [128, 16])
        w_x_t = din("w_x_t", [DIN, 96])
        b_x_c = din("b_x_c", [96, 1])
        w_dt_t = din("w_dt_t", [DTR, DIN])
        b_dt_r = din("b_dt_r", [128, 16])
        a_cols = din("a_cols", [128, 256])
        d_par_r = din("d_par_r", [128, 16])
        self.w_out_t = din("w_out_t", [DIN, D])
        b_out_c = din("b_out_c", [128, 8])
        ones_c = din("ones_c", [128, 1])
        gate_t = din("gate_t", [D, 8])
        corr_r = din("corr_r", [1, 8])
        self.fc1_t = din("fc1_t", [D, 512])
        self.w1f = din("w1f", [8 * 512, 512])
        self.w2f = din("w2f", [8 * 512, 512])
        self.fc2_t = din("fc2_t", [512, D])
        self.up_t = din("up_t", [D, 4096])
        self.down_t = din("down_t", [4096, D])
        eye_c = din("eye_c", [128, 128])
        self.qw_ = din("qw_", [D, D])
        self.kw_ = din("kw_", [D, D])
        self.vw_ = din("vw_", [D, D])
        self.ow_ = din("ow_", [D, D])
        self.bw_ = din("bw_", [D, 8])
        self.mixw_ = din("mixw_", [D, 8])
        mixb_c = din("mixb_c", [8, 1])
        mixbias_c = din("mixbias_c", [8, 1])
        ctaps_c = din("ctaps_c", [128, 96])
        wnorm_c = din("wnorm_c", [128, 1])
        lmask_c = din("lmask_c", [128, 128])
        umask_c = din("umask_c", [128, 128])
        uimask_c = din("uimask_c", [128, 128])
        self.x_out = nc.dram_tensor("x_out", [D, L], F32, kind="ExternalOutput").ap()
        self.xres = nc.dram_tensor("xres", [D, L], F32, kind="Internal").ap()
        self.q_s = nc.dram_tensor("q_s", [D, L], F32, kind="Internal").ap()
        self.k_s = nc.dram_tensor("k_s", [D, L], F32, kind="Internal").ap()
        self.kb_s = nc.dram_tensor("kb_s", [D, L], F32, kind="Internal").ap()
        self.vb_s = nc.dram_tensor("vb_s", [D, L], F32, kind="Internal").ap()
        self.v_s = nc.dram_tensor("v_s", [D, L], F32, kind="Internal").ap()
        self.o_s = nc.dram_tensor("o_s", [D, L], F32, kind="Internal").ap()
        self.bounce = nc.dram_tensor("bounce", [80, L], F32, kind="Internal").ap()
        self.zsilu = nc.dram_tensor("zsilu", [DIN, L], F32, kind="Internal").ap()

        self.G1 = sbuf("G1", 8 * T)
        self.GH = sbuf("GH", 16 * (T + 3))
        self.G3 = sbuf("G3", 16 * T)
        self.G4 = sbuf("G4", 16 * T)
        self.G5 = sbuf("G5", 16 * T)
        self.WB = sbuf("WB", 2 * 1024)
        self.XD = sbuf("XD", T)
        self.SLT = sbuf("SLT", 8 * T)
        self.ZST = sbuf("ZST", 2 * T)
        self.LNB = sbuf("LNB", 2 * T)
        self.BCB = sbuf("BCB", 2 * T)
        self.ROWQ = sbuf("ROWQ", 3 * T, rows=1)
        self.HC = sbuf("HC", 256)
        self.HALO = sbuf("HALO", 72)
        self.CON = sbuf("CON", 432)
        self.WXT = sbuf("WXT", 16 * 96)
        self.WDT = sbuf("WDT", 16 * 128)
        self.GATE = sbuf("GATE", 64)
        self.IDT = sbuf("IDT", 128)
        self.LMSK = sbuf("LMSK", 128)
        self.UMSK = sbuf("UMSK", 128)
        self.UIMSK = sbuf("UIMSK", 128)
        self.ATC = sbuf("ATC", 100)

        CON = self.CON
        self.ONES = CON[:, 0:1]
        self.ACOL = lambda g, s: CON[:, 1 + 16 * g + s: 2 + 16 * g + s]
        self.DCOL = lambda g: CON[:, 257 + g: 258 + g]
        self.TAP = lambda g, j: CON[:, 273 + 4 * g + j: 274 + 4 * g + j]
        self.CBCOL = lambda g: CON[:, 337 + g: 338 + g]
        self.BDTCOL = lambda g: CON[:, 353 + g: 354 + g]
        self.BINCOL = lambda m: CON[:, 369 + m: 370 + m]
        self.BXCOL = CON[:96, 401:402]
        self.BOUTCOL = lambda m: CON[:, 402 + m: 403 + m]

        def brow(r, t0, n, np_=128):
            return bass.AP(self.bounce.tensor, r * L + t0, [[0, np_], [1, n]])
        self.brow = brow

        p.dma("sync", self.ONES, ones_c[:, :], stream="ini")
        p.dma("sync", self.SLT[:, 0:256], a_cols[:, :], stream="ini")
        p.dma("sync", CON[:, 257:273], d_par_r[:, :], stream="ini")
        p.dma("sync", CON[:, 273:337], conv_w_r[:, :], stream="ini")
        p.dma("sync", CON[:, 337:353], conv_b_r[:, :], stream="ini")
        p.dma("sync", CON[:, 353:369], b_dt_r[:, :], stream="ini")
        p.dma("sync", CON[:, 369:401], b_in_c[:, :], stream="ini")
        p.dma("sync", self.BXCOL, b_x_c[:, :], stream="ini")
        p.dma("sync", CON[:, 402:410], b_out_c[:, :], stream="ini")
        p.dma("sync", self.WXT[:, :],
              bass.AP(w_x_t.tensor, 0, [[96, 128], [128 * 96, 16], [1, 96]]),
              stream="ini")
        p.dma("sync", self.WDT[0:64, :],
              bass.AP(w_dt_t.tensor, 0, [[DIN, 64], [1, 16 * 128]]),
              stream="ini")
        p.dma("sync", self.GATE[:, :],
              bass.AP(gate_t.tensor, 0, [[8, 128], [1024, 8], [1, 8]]),
              stream="ini")
        p.dma("sync", self.IDT[:, :], eye_c[:, :], stream="ini")
        p.dma("sync", CON[:, 410:418],
              bass.AP(corr_r.tensor, 0, [[0, 128], [1, 8]]), stream="ini")
        p.dma("sync", self.LMSK[:, :], lmask_c[:, :], stream="ini")
        p.dma("sync", self.UMSK[:, :], umask_c[:, :], stream="ini")
        p.dma("sync", self.UIMSK[:, :], uimask_c[:, :], stream="ini")
        p.dma("sync", self.ATC[0:8, 0:1], mixb_c[:, :], stream="ini")
        p.dma("sync", self.ATC[0:8, 1:2], mixbias_c[:, :], stream="ini")
        p.dma("sync", self.ATC[:, 2:3], wnorm_c[:, :], stream="ini")
        ini = p.dma("sync", self.ATC[:, 4:100], ctaps_c[:, :], stream="ini")
        self.MIXB = self.ATC[0:8, 0:1]
        self.WNORM = self.ATC[:, 2:3]
        self.CTAP = lambda ti, m, j: self.ATC[:, 4 + 32 * ti + 4 * m + j:
                                              5 + 32 * ti + 4 * m + j]
        self.mixb_tok = p.run("vector", lambda e: e.tensor_add(
            self.ATC[0:8, 0:1], self.ATC[0:8, 0:1], self.ATC[0:8, 1:2]),
            waits=[ini], inc=True)
        self.ini_tok = ini
        self.wxt_tok = ini
        self.wdt_tok = ini
        self.CORR = CON[:, 410:418]
        self.ONES8 = CON[:, 418:426]
        p.run("vector", lambda e: e.memset(CON[:, 418:426], 1.0))
        t_e = p.run("scalar", lambda e: e.activation(CON[:, 1:257], self.SLT[:, 0:256],
                                                     AF.Exp), waits=[ini], inc=True)
        self.t_acol = p.run("vector", lambda e: e.tensor_scalar_mul(
            CON[:, 1:257], CON[:, 1:257], -1.0), waits=[t_e], inc=True)
        t_hc = p.run("vector", lambda e: e.memset(self.HC[:, :], 0.0), inc=True)
        self.t_zero = t_hc
        self.t_g3w = t_hc
        self.bcb_free = [None, None]
        self.prev_tb = (None, None)

        self.store_toks = []
        dst1 = self.x_out if self.nsub == 1 else self.xres
        self.emit_mamba(self.x_in, dst1)
        if self.nsub >= 2:
            prev = self.store_toks
            self.store_toks = []
            dst2 = self.x_out if self.nsub == 2 else self.xres
            self.emit_ffn(self.xres, dst2, is_last=(self.nsub == 2), tag="a",
                          src_ready=prev)
        if self.nsub >= 3:
            prev = self.store_toks
            self.store_toks = []
            dst3 = self.x_out if self.nsub == 3 else self.xres
            self.emit_attn(self.xres, dst3, is_last=(self.nsub == 3),
                           src_ready=prev)
        if self.nsub >= 4:
            prev = self.store_toks
            self.store_toks = []
            self.emit_ffn(self.xres, self.x_out, is_last=True, tag="b",
                          src_ready=prev)

        self.p.finalize(self.final_tokens)


_PROG_CACHE = {}


def _get_prog(nsub):
    if nsub not in _PROG_CACHE:
        _PROG_CACHE[nsub] = Prog(nsub)
    return _PROG_CACHE[nsub]


def _host_layout(inputs):
    x = np.ascontiguousarray(inputs["L0_attn_a_x"])
    w = {}
    w["w_in_t"] = np.ascontiguousarray(inputs["L0_attn_a_W_in"].T)
    w["b_in_c"] = np.ascontiguousarray(inputs["L0_attn_a_b_in"].reshape(32, 128).T)
    w["conv_w_r"] = np.ascontiguousarray(
        inputs["L0_attn_a_conv_w"].reshape(DIN, 4).reshape(16, 128, 4)
        .transpose(1, 0, 2).reshape(128, 64))
    w["conv_b_r"] = np.ascontiguousarray(inputs["L0_attn_a_conv_b"].reshape(16, 128).T)
    w["w_x_t"] = np.ascontiguousarray(inputs["L0_attn_a_W_x"].T)
    w["b_x_c"] = np.ascontiguousarray(inputs["L0_attn_a_b_x"].reshape(96, 1))
    w["w_dt_t"] = np.ascontiguousarray(inputs["L0_attn_a_W_dt"].T)
    w["b_dt_r"] = np.ascontiguousarray(inputs["L0_attn_a_b_dt"].reshape(16, 128).T)
    w["a_cols"] = np.ascontiguousarray(
        inputs["L0_attn_a_A_log"].reshape(16, 128, 16).transpose(1, 0, 2).reshape(128, 256))
    w["d_par_r"] = np.ascontiguousarray(inputs["L0_attn_a_D_param"].reshape(16, 128).T)
    w["w_out_t"] = np.ascontiguousarray(inputs["L0_attn_a_W_out"].T)
    w["b_out_c"] = np.ascontiguousarray(inputs["L0_attn_a_b_out"].reshape(8, 128).T)
    w["ones_c"] = np.ones((128, 1), np.float32)
    w["gate_t"] = np.ascontiguousarray(inputs["L0_ffn_gate"].T)
    w["corr_r"] = np.ascontiguousarray(inputs["L0_ffn_corr_bias"].reshape(1, 8))
    w["fc1_t"] = np.ascontiguousarray(inputs["L0_ffn_fc1_w"].T)
    w["w1f"] = np.ascontiguousarray(
        inputs["L0_ffn_w1"].transpose(0, 2, 1).reshape(8 * 512, 512))
    w["w2f"] = np.ascontiguousarray(
        inputs["L0_ffn_w2"].transpose(0, 2, 1).reshape(8 * 512, 512))
    w["fc2_t"] = np.ascontiguousarray(inputs["L0_ffn_fc2_w"].T)
    w["up_t"] = np.ascontiguousarray(inputs["L0_ffn_up_w"].T)
    w["down_t"] = np.ascontiguousarray(inputs["L0_ffn_down_w"].T)
    w["eye_c"] = np.eye(128, dtype=np.float32)
    w["qw_"] = np.ascontiguousarray(inputs["L0_attn_b_q_w"])
    w["kw_"] = np.ascontiguousarray(inputs["L0_attn_b_k_w"])
    w["vw_"] = np.ascontiguousarray(inputs["L0_attn_b_v_w"])
    w["ow_"] = np.ascontiguousarray(inputs["L0_attn_b_o_w"])
    w["bw_"] = np.ascontiguousarray(inputs["L0_attn_b_b_w"])
    w["mixw_"] = np.ascontiguousarray(inputs["L0_attn_b_mix_w"])
    w["mixb_c"] = np.ascontiguousarray(inputs["L0_attn_b_mix_b"].reshape(8, 1))
    w["mixbias_c"] = np.ascontiguousarray(inputs["L0_attn_b_mix_bias"].reshape(8, 1))
    taps = []
    for kk in ("L0_attn_b_conv_q_w", "L0_attn_b_conv_k_w", "L0_attn_b_conv_v_w"):
        taps.append(inputs[kk].reshape(8, 128, 4).transpose(1, 0, 2).reshape(128, 32))
    w["ctaps_c"] = np.ascontiguousarray(np.concatenate(taps, axis=1))
    w["wnorm_c"] = np.ascontiguousarray(inputs["L0_attn_b_o_norm_w"].reshape(128, 1))
    on = np.ones((128, 128), np.float32)
    w["lmask_c"] = np.ascontiguousarray(-np.tril(on, -1))
    w["umask_c"] = np.ascontiguousarray(-np.triu(on, 1))
    w["uimask_c"] = np.ascontiguousarray(np.triu(on, 0))
    maps = []
    for b in range(8):
        m = {"x_in": np.ascontiguousarray(x[b].T)}
        m.update(w)
        maps.append(m)
    return maps


def kernel(**inputs):
    prog = _get_prog(N_SUBLAYERS)
    maps = _host_layout(inputs)
    res = run_bass_kernel_spmd(prog.nc, maps, core_ids=list(range(8)))
    out = np.stack([np.asarray(r["x_out"]).T for r in res.results])
    return out.astype(np.float32)
